# version 09
# baseline (speedup 1.0000x reference)
import contextlib
from collections import defaultdict

import numpy as np
import ml_dtypes

import concourse.bass as bass
import concourse.mybir as mybir
from concourse.bass_utils import run_bass_kernel_spmd

F32 = mybir.dt.float32
BF16 = mybir.dt.bfloat16
AF = mybir.ActivationFunctionType
ALU = mybir.AluOpType
AX = mybir.AxisListType

T, D, FF, NE = 2048, 1024, 2048, 8
NT, NTH, NPAIR = 16, 8, 8
CAP = 640
NSLOT = NE * CAP
LNEPS, ZEPS = 1e-5, 1e-6
I32 = mybir.dt.int32

ENG_SEM = {"tensor": "sPE", "vector": "sVE", "scalar": "sAC", "gpsimd": "sPO"}


class P:
    def __init__(self, nc):
        self.nc = nc
        self.q = {k: [] for k in ("sync", "tensor", "vector", "scalar", "gpsimd")}
        self.sem = {}
        self.c = defaultdict(int)

    def mksem(self, ctx, *names):
        for n in names:
            self.sem[n] = ctx.enter_context(self.nc.semaphore(n))

    def wait(self, eng, sem, val=None):
        v = self.c[sem] if val is None else val
        if v <= 0:
            return
        h = self.sem[sem]
        self.q[eng].append(lambda E: E.wait_ge(h, v))

    def op(self, eng, fn, inc=None, n=1):
        if inc is None:
            self.q[eng].append(fn)
            return None
        self.c[inc] += n
        h = self.sem[inc]
        self.q[eng].append(lambda E: fn(E).then_inc(h, n))
        return self.c[inc]

    def gop(self, eng, fn):
        v = self.op(eng, fn, inc=ENG_SEM[eng])
        self.wait(eng, ENG_SEM[eng], v)
        return v

    def wait_marks(self, eng, marks):
        for sem, val in marks:
            self.wait(eng, sem, val)

    def dma(self, eng, out, in_, sem):
        return self.op(eng, lambda E: E.dma_start(out=out, in_=in_), inc=sem, n=16)


class CastStream:

    def __init__(self, p, slots, slot_sems, cast_sems, engines):
        self.p, self.slots, self.slot_sems, self.engines = p, slots, slot_sems, engines
        self.cast_sems = cast_sems
        self.k = 0
        self.kc = 0
        self.pending = []

    def load(self, src_ap, dst_ap):
        s = self.k % len(self.slots)
        if self.k >= len(self.slots):
            self.p.wait("sync", self.cast_sems[s], self.k // len(self.slots))
        self.p.dma("sync", self.slots[s], src_ap, self.slot_sems[s])
        self.pending.append((self.k, dst_ap, self.p.c[self.slot_sems[s]]))
        self.k += 1

    def drain(self, count=None, dst_free=None):
        p = self.p
        todo = self.pending if count is None else self.pending[:count]
        self.pending = self.pending[len(todo):]
        for idx, dst, dval in todo:
            eng = self.engines[self.kc % len(self.engines)]
            s = idx % len(self.slots)
            p.wait(eng, self.slot_sems[s], dval)
            if dst_free is not None:
                p.wait(eng, dst_free[0], dst_free[1])
            src = self.slots[s]
            if eng == "scalar":
                p.op(eng, lambda E, d=dst, sp=src: E.copy(out=d, in_=sp),
                     inc=self.cast_sems[s])
            else:
                p.op(eng, lambda E, d=dst, sp=src: E.tensor_copy(out=d, in_=sp),
                     inc=self.cast_sems[s])
            self.kc += 1
        return [(cs, p.c[cs]) for cs in self.cast_sems]


def build_program():
    nc = bass.Bass("TRN2", target_bir_lowering=False)

    def din(name, shape, dt=F32):
        return nc.dram_tensor(name, shape, dt, kind="ExternalInput").ap()

    x_in = din("x", [T, D])
    y_out = nc.dram_tensor("y", [T, D], F32, kind="ExternalOutput").ap()
    wqT, wkT, wvT, woT = (din(n, [D, D]) for n in ("wqT", "wkT", "wvT", "woT"))
    cbias = din("cbias", [4, D])
    router = din("router", [D, NE])
    w_in_e, w_sw_e = din("w_in_e", [NE, D, FF]), din("w_sw_e", [NE, D, FF])
    w_out_e = din("w_out_e", [NE, FF, D])
    w_in_s, w_sw_s = din("w_in_s", [D, FF]), din("w_sw_s", [D, FF])
    w_out_s = din("w_out_s", [FF, D])
    c_id_bf = din("c_id_bf", [128, 128], BF16)
    c_ones_bf = din("c_ones_bf", [1, 512], BF16)
    c_selp = din("c_selp", [16, 1024], F32)
    c_tri = din("c_tri", [128, 128], BF16)
    c_onesf = din("c_onesf", [1, 128], F32)
    c_tok = din("c_tok", [128, 16], I32)
    c_thr = din("c_thr", [1, 8], F32)
    c_base = din("c_base", [1, 8], F32)
    c_zidx = din("c_zidx", [1, NSLOT], I32)
    ln2d = nc.dram_tensor("ln2d", [T, D], BF16, kind="Internal").ap()
    gatd = nc.dram_tensor("gatd", [T, NE], F32, kind="Internal").ap()
    idxd = nc.dram_tensor("idxd", [NSLOT], I32, kind="Internal").ap()
    poutd = nc.dram_tensor("poutd", [NSLOT + 1, D], F32, kind="Internal").ap()
    xmid = nc.dram_tensor("xmid", [T, D], F32, kind="Internal").ap()

    with contextlib.ExitStack() as ctx:
        E_ = ctx.enter_context
        xh = E_(nc.sbuf_tensor("xh", [128, 8192], F32))
        lnT = E_(nc.sbuf_tensor("lnT", [128, 8192], BF16))
        qh = E_(nc.sbuf_tensor("qh", [128, 8192], BF16))
        wbfA = E_(nc.sbuf_tensor("wbfA", [128, 16384], BF16))
        wbfB = E_(nc.sbuf_tensor("wbfB", [128, 16384], BF16))
        wod = E_(nc.sbuf_tensor("wod", [128, 16384], BF16))
        landA = E_(nc.sbuf_tensor("landA", [128, 4096], F32))
        gbuf = E_(nc.sbuf_tensor("gbuf", [128, 1024], F32))
        m2 = E_(nc.sbuf_tensor("m2", [128, 1024], F32))
        scr8 = E_(nc.sbuf_tensor("scr8", [128, 2048], F32))
        kvb = E_(nc.sbuf_tensor("kvb", [128, 1024], BF16))
        ksel = E_(nc.sbuf_tensor("ksel", [128, 128], BF16))
        biasr = E_(nc.sbuf_tensor("biasr", [65, 1024], BF16))
        onesb = E_(nc.sbuf_tensor("onesb", [65, 512], BF16))
        identb = E_(nc.sbuf_tensor("identb", [128, 128], BF16))
        routerf = E_(nc.sbuf_tensor("routerf", [128, 64], F32))
        routerb = E_(nc.sbuf_tensor("routerb", [128, 64], BF16))
        bo_row = E_(nc.sbuf_tensor("bo_row", [1, 1024], BF16))
        tri_sb = E_(nc.sbuf_tensor("tri_sb", [128, 128], BF16))
        onesf = E_(nc.sbuf_tensor("onesf", [1, 128], F32))
        tok_sb = E_(nc.sbuf_tensor("tok_sb", [128, 16], I32))
        onescol = E_(nc.sbuf_tensor("onescol", [128, 1], BF16))
        thr_b = E_(nc.sbuf_tensor("thr_b", [128, 8], F32))
        base_row = E_(nc.sbuf_tensor("base_row", [1, 8], F32))
        posf = E_(nc.sbuf_tensor("posf", [128, 8], F32))
        posi = E_(nc.sbuf_tensor("posi", [128, 8], I32))
        idxs = E_(nc.sbuf_tensor("idxs", [128, 40], I32))
        invs = E_(nc.sbuf_tensor("invs", [128, 32], I32))
        invf = E_(nc.sbuf_tensor("invf", [128, 2], F32))
        st1 = E_(nc.sbuf_tensor("st1", [128, 32], F32))
        st2 = E_(nc.sbuf_tensor("st2", [128, 16], F32))
        bn12w = E_(nc.sbuf_tensor("bn12w", [128, 96], F32))
        bn12 = bn12w[:, 0:12]
        mvv = E_(nc.sbuf_tensor("mvv", [128, 2], F32))
        scall = E_(nc.sbuf_tensor("scall", [128, 64], F32))
        mskb = E_(nc.sbuf_tensor("mskb", [128, 64], F32))
        mxb = E_(nc.sbuf_tensor("mxb", [128, 16], F32))
        gatb = E_(nc.sbuf_tensor("gatb", [128, 64], F32))
        gatbb = E_(nc.sbuf_tensor("gatbb", [128, 64], BF16))
        PS = E_(nc.psum_tensor("PS", [128, 4096], F32))

        bc_cell = {}
        p = P(nc)
        p.mksem(ctx, "sLXa", "sLXb", "sW0", "sW1", "sW2", "sW3",
                "sC0", "sC1", "sC2", "sC3", "sLC", "sSELP", "sGF", "sIX", "sST",
                "sLNa", "sLNb", "sSC", "sGA", "sGB", "sGC", "sGD", "sPa", "sPb",
                "sPE", "sVE", "sAC", "sPO")

        def bank(b):
            return PS[:, 512 * b:512 * (b + 1)]

        def bank_bf(b):
            return PS[:, 512 * b:512 * (b + 1)].bitcast(BF16)

        elu_scr = [scr8[:, 0:1024], scr8[:, 1024:2048]]
        zbuf = scr8[:, 1024:2048]
        rt = scr8.bitcast(BF16)
        zr1 = m2[0:16, :]
        zr2 = scr8[0:16, 0:1024]
        selp = gbuf[0:16, :]
        gslot = [gbuf.bitcast(BF16)[:, 0:512], gbuf.bitcast(BF16)[:, 512:1024]]
        m2s = [m2[:, 0:512], m2[:, 512:1024]]
        kxs = [qh[:, 0:1024], qh[:, 1024:2048]]
        vxs = [qh[:, 2048:3088], qh[:, 3088:4128]]
        aT = qh[:, 4160:5184]
        xa32 = lnT.bitcast(F32)
        lnts = [qh[:, 5184:6208], qh[:, 6208:7232]]
        cb32 = landA[0:65, 0:1024]
        cbo = landA[0:1, 1024:2048]
        selm = bn12w[:, 0:64]
        kv_bank = [0, 0, 0, 1, 1, 1, 2, 2]
        kv_col = [0, 130, 260, 0, 130, 260, 0, 130]

        wcs = CastStream(p, [landA[:, 1024 * s:1024 * (s + 1)] for s in range(4)],
                         ["sW0", "sW1", "sW2", "sW3"], ["sC0", "sC1", "sC2", "sC3"],
                         ["vector", "scalar"])
        ocs = wcs

        p.dma("sync", identb[:], c_id_bf[:], "sLC")
        p.dma("sync", onesb[0:1, :], c_ones_bf[:], "sLC")
        p.dma("sync", onesb[32:33, :], c_ones_bf[:], "sLC")
        p.dma("sync", onesb[64:65, :], c_ones_bf[:], "sLC")
        p.dma("sync", cb32[0:1, :], cbias[0:1, :], "sLC")
        p.dma("sync", cb32[32:33, :], cbias[1:2, :], "sLC")
        p.dma("sync", cb32[64:65, :], cbias[2:3, :], "sLC")
        p.dma("sync", cbo[:], cbias[3:4, :], "sLC")
        for c in range(8):
            p.dma("sync", routerf[:, 8 * c:8 * c + 8], router[128 * c:128 * (c + 1), :], "sLC")
        p.dma("sync", tri_sb[:], c_tri[:], "sLC")
        p.dma("sync", onesf[:], c_onesf[:], "sLC")
        p.dma("sync", tok_sb[:], c_tok[:], "sLC")
        p.dma("sync", onescol[:], c_ones_bf[0:1, 0:128].rearrange("o p -> p o"), "sLC")
        p.dma("sync", thr_b[0:1, :], c_thr[:], "sLC")
        p.dma("sync", base_row[:], c_base[:], "sLC")
        p.dma("sync", idxd.rearrange("(a b) -> a b", a=1), c_zidx[:], "sLC")
        p.dma("sync", poutd[NSLOT:NSLOT + 1, :], c_zidx[0:1, 0:1024].bitcast(F32), "sLC")
        vLC = p.c["sLC"]
        p.wait("vector", "sLC", vLC)
        p.op("vector", lambda E: E.tensor_copy(out=biasr[0:1, 0:1024], in_=cb32[0:1, :]), inc="sVE")
        p.op("vector", lambda E: E.tensor_copy(out=biasr[32:33, 0:1024], in_=cb32[32:33, :]), inc="sVE")
        p.op("vector", lambda E: E.tensor_copy(out=biasr[64:65, 0:1024], in_=cb32[64:65, :]), inc="sVE")
        p.op("vector", lambda E: E.tensor_copy(out=bo_row[:], in_=cbo), inc="sVE")
        vBIAS = p.op("vector", lambda E: E.tensor_copy(out=routerb[:], in_=routerf[:]), inc="sVE")
        p.wait("tensor", "sLC", vLC)
        vthr = p.op("tensor", lambda E: E.matmul(
            out=PS[:, 0:8], lhsT=onesf[0:1, :], rhs=thr_b[0:1, :],
            start=True, stop=True), inc="sPE")
        p.wait("vector", "sPE", vthr)
        vthr2 = p.op("vector", lambda E: E.tensor_copy(out=thr_b[:], in_=PS[:, 0:8]), inc="sVE")
        p.op("gpsimd", lambda E: E.memset(kvb[:], 0.0), inc="sPO")
        p.op("gpsimd", lambda E: E.memset(ksel[:], 0.0), inc="sPO")
        for s in range(2):
            p.op("gpsimd",
                 lambda E, s=s: E.memset(vxs[s].rearrange("p (g c) -> p g c", c=65)[:, :, 64:65], 1.0),
                 inc="sPO")
        vMEMS = p.c["sPO"]

        def load_attn_weights(full):
            mats = ((wkT, wbfA[:, 8192:16384]), (wvT, wbfB[:, 0:8192])) if full else ()
            mats = mats + ((wqT, wbfA[:, 0:8192]), (woT, wbfB[:, 8192:16384]))
            for src, dst in mats:
                for d in range(8):
                    wcs.load(src[128 * d:128 * (d + 1), :], dst[:, 1024 * d:1024 * (d + 1)])

        def load_expert_insw(e):
            for hc in range(2):
                for mat_e, mat_s, dst in ((w_in_e, w_in_s, wbfA), (w_sw_e, w_sw_s, wbfB)):
                    for d in range(8):
                        if e < NE:
                            sl = mat_e[e, 128 * d:128 * (d + 1), 1024 * hc:1024 * (hc + 1)]
                        else:
                            sl = mat_s[128 * d:128 * (d + 1), 1024 * hc:1024 * (hc + 1)]
                        wcs.load(sl, dst[:, 2048 * d + 1024 * hc:2048 * d + 1024 * (hc + 1)])

        def load_expert_wout(e):
            for f in range(16):
                sl = (w_out_e[e, 128 * f:128 * (f + 1), :] if e < NE
                      else w_out_s[128 * f:128 * (f + 1), :])
                ocs.load(sl, wod[:, 1024 * f:1024 * (f + 1)])

        def ln_stats(xtile_ap, stats_ap):
            p.gop("vector", lambda E: E.bn_stats(out=bn12[:, 0:6], in_=xtile_ap[:, 0:512]))
            p.gop("vector", lambda E: E.bn_stats(out=bn12[:, 6:12], in_=xtile_ap[:, 512:1024]))
            p.gop("vector", lambda E: E.bn_aggr(out=mvv[:], in_=bn12[:]))
            p.gop("vector", lambda E: E.tensor_scalar(
                out=stats_ap[:, 0:1], in0=mvv[:, 0:1], scalar1=-1.0, scalar2=None, op0=ALU.mult))
            vv = p.gop("vector", lambda E: E.tensor_scalar(
                out=mvv[:, 1:2], in0=mvv[:, 1:2], scalar1=LNEPS, scalar2=None, op0=ALU.add))
            p.wait("scalar", "sVE", vv)
            va = p.gop("scalar", lambda E: E.activation(out=mvv[:, 1:2], in_=mvv[:, 1:2], func=AF.Ln))
            va = p.gop("scalar", lambda E, s=stats_ap: E.activation(
                out=s[:, 1:2], in_=mvv[:, 1:2], func=AF.Exp, scale=-0.5))
            p.wait("vector", "sAC", va)

        p.wait("sync", "sVE", vBIAS)
        load_attn_weights(full=True)
        cw_kv = wcs.drain(16)
        wcs.drain()
        kv_mark = {}
        vn_mark = {}
        for t in range(NT):
            if t < 8:
                xt = xh[:, 1024 * t:1024 * (t + 1)]
            else:
                xt = xa32[:, 1024 * (t % 2):1024 * (t % 2 + 1)]
            sx = "sLXa" if t % 2 == 0 else "sLXb"
            if t >= 2:
                p.wait("sync", "sVE", vn_mark[t - 2])
            p.dma("sync", xt, x_in[128 * t:128 * (t + 1), :], sx)
            p.wait("vector", sx, p.c[sx])
            stat = st1[:, 2 * t:2 * t + 2]
            ln_stats(xt, stat)
            lt = lnts[t % 2]
            vn = p.op("vector", lambda E, xt=xt, lt=lt, s=stat: E.tensor_scalar(
                out=lt, in0=xt, scalar1=s[:, 0:1], scalar2=s[:, 1:2],
                op0=ALU.add, op1=ALU.mult), inc="sVE")
            vn_mark[t] = vn
            p.wait("tensor", "sVE", vn)
            for c in range(8):
                vtp = p.op("tensor", lambda E, c=c, lt=lt: E.transpose(
                    out=bank_bf(3)[:, 128 * c:128 * (c + 1)], in_=lt[:, 128 * c:128 * (c + 1)],
                    identity=identb[:]), inc="sPE" if c == 7 else None)
            p.wait("vector", "sPE", vtp)
            vev = p.op("vector", lambda E: E.tensor_copy(out=aT, in_=bank_bf(3)[:]), inc="sVE")

            p.wait("tensor", "sVE", vev)
            if t == 0:
                p.wait_marks("tensor", cw_kv)
                p.wait("tensor", "sVE", vthr2)
                p.wait("tensor", "sPO", vMEMS)
            for h2 in range(2):
                for d in range(8):
                    p.op("tensor", lambda E, d=d, h2=h2: E.matmul(
                        out=bank(4 + h2), lhsT=aT[:, 128 * d:128 * (d + 1)],
                        rhs=wbfA[:, 8192 + 1024 * d + 512 * h2:8192 + 1024 * d + 512 * (h2 + 1)],
                        start=(d == 0), stop=False))
                p.op("tensor", lambda E, h2=h2: E.matmul(
                    out=bank(4 + h2), lhsT=onesb[32:33, 0:128],
                    rhs=biasr[32:33, 512 * h2:512 * (h2 + 1)], start=False, stop=True))
                for d in range(8):
                    p.op("tensor", lambda E, d=d, h2=h2: E.matmul(
                        out=bank(6 + h2), lhsT=aT[:, 128 * d:128 * (d + 1)],
                        rhs=wbfB[:, 1024 * d + 512 * h2:1024 * d + 512 * (h2 + 1)],
                        start=(d == 0), stop=False))
                vkv = p.op("tensor", lambda E, h2=h2: E.matmul(
                    out=bank(6 + h2), lhsT=onesb[64:65, 0:128],
                    rhs=biasr[64:65, 512 * h2:512 * (h2 + 1)], start=False, stop=True),
                    inc="sPE" if h2 == 1 else None)

            kx, vxt, scr = kxs[t % 2], vxs[t % 2], elu_scr[t % 2]
            p.wait("vector", "sPE", vkv)
            if t >= 2:
                p.wait("vector", "sPE", kv_mark[t - 2])
            v1 = p.op("vector", lambda E, scr=scr: E.tensor_single_scalar(
                out=scr, in_=PS[:, 2048:3072], scalar=0.0, op=ALU.min), inc="sVE")
            p.wait("scalar", "sVE", v1)
            v2 = p.op("scalar", lambda E, scr=scr: E.activation(out=scr, in_=scr, func=AF.Exp),
                      inc="sAC")
            p.wait("vector", "sAC", v2)
            p.op("vector", lambda E, kx=kx, scr=scr: E.scalar_tensor_tensor(
                out=kx, in0=PS[:, 2048:3072], scalar=0.0, in1=scr,
                op0=ALU.max, op1=ALU.add), inc="sVE")
            vv2 = p.op("vector", lambda E, vxt=vxt: E.tensor_copy(
                out=vxt.rearrange("p (g c) -> p g c", c=65)[:, :, 0:64],
                in_=PS[:, 3072:4096].rearrange("p (g c) -> p g c", c=64)), inc="sVE")

            p.wait("tensor", "sVE", vv2)
            for pr in range(NPAIR):
                b, co = kv_bank[pr], kv_col[pr]
                st_ = (t == 0) and (pr in (0, 3, 6))
                sp_ = (t == NT - 1) and (pr in (2, 5, 7))
                kv_mark[t] = p.op("tensor", lambda E, pr=pr, b=b, co=co, kx=kx, vxt=vxt,
                                  st_=st_, sp_=sp_: E.matmul(
                    out=PS[:, 512 * b + co:512 * b + co + 130],
                    lhsT=kx[:, 128 * pr:128 * (pr + 1)],
                    rhs=vxt[:, 130 * pr:130 * (pr + 1)],
                    start=st_, stop=sp_), inc="sPE" if pr == NPAIR - 1 else None)

        p.wait("vector", "sPE", kv_mark[NT - 1])
        for pr in range(NPAIR):
            base = 512 * kv_bank[pr] + kv_col[pr]
            p.op("vector", lambda E, pr=pr, base=base: E.tensor_copy(
                out=kvb[0:64, 128 * pr:128 * pr + 64], in_=PS[0:64, base:base + 64]), inc="sVE")
            p.op("vector", lambda E, pr=pr, base=base: E.tensor_copy(
                out=kvb[64:128, 128 * pr + 64:128 * pr + 128],
                in_=PS[64:128, base + 65:base + 129]), inc="sVE")
            p.op("vector", lambda E, pr=pr, base=base: E.tensor_copy(
                out=ksel[0:64, 18 * pr:18 * pr + 1], in_=PS[0:64, base + 64:base + 65]), inc="sVE")
            p.op("vector", lambda E, pr=pr, base=base: E.tensor_copy(
                out=ksel[64:128, 18 * pr + 1:18 * pr + 2],
                in_=PS[64:128, base + 129:base + 130]), inc="sVE")
        vKVB = p.c["sVE"]

        for hf in range(2):
            toff = NTH * hf
            if hf == 1:
                p.wait("sync", "sPO", p.c["sPO"])
                p.wait("sync", "sVE", p.c["sVE"])
                p.wait("sync", "sPE", p.c["sPE"])
            p.dma("sync", selp[:], c_selp[:], "sSELP")
            vSELP = p.c["sSELP"]
            if hf == 1:
                for i in range(NTH):
                    t = toff + i
                    p.dma("sync", xh[:, 1024 * i:1024 * (i + 1)],
                          x_in[128 * t:128 * (t + 1), :], "sLXa" if i % 2 == 0 else "sLXb")
                load_attn_weights(full=False)
            vXHa, vXHb = p.c["sLXa"], p.c["sLXb"]

            dstf = ("sPE", p.c["sPE"]) if hf == 1 else None
            load_expert_insw(NE)
            load_expert_wout(NE)

            p.wait("vector", "sLXa", vXHa)
            p.wait("vector", "sLXb", vXHb)
            for i in range(NTH):
                t = toff + i
                stat = st1[:, 2 * t:2 * t + 2]
                lt = lnts[i % 2]
                vn = p.op("vector", lambda E, i=i, lt=lt, s=stat: E.tensor_scalar(
                    out=lt, in0=xh[:, 1024 * i:1024 * (i + 1)], scalar1=s[:, 0:1],
                    scalar2=s[:, 1:2], op0=ALU.add, op1=ALU.mult), inc="sVE")
                p.wait("tensor", "sVE", vn)
                bb = 3 if i % 2 == 0 else 1
                for c in range(8):
                    vtp = p.op("tensor", lambda E, c=c, lt=lt, bb=bb: E.transpose(
                        out=bank_bf(bb)[:, 128 * c:128 * (c + 1)],
                        in_=lt[:, 128 * c:128 * (c + 1)], identity=identb[:]),
                        inc="sPE" if c == 7 else None)
                p.wait("vector", "sPE", vtp)
                p.op("vector", lambda E, i=i, bb=bb: E.tensor_copy(
                    out=lnT.rearrange("p (c ti q) -> p c ti q", c=8, ti=8)[:, :, i, :],
                    in_=bank_bf(bb)[:].rearrange("p (c q) -> p c q", c=8)), inc="sVE")
                if hf == 1:
                    wcs.drain(2, dst_free=dstf)
            vLN1T = p.c["sVE"]
            cwA = [(cs, p.c[cs]) for cs in wcs.cast_sems]

            p.wait("tensor", "sVE", vLN1T)
            p.wait_marks("tensor", cwA)
            vq_evac = {}
            for o in range(8):
                g0 = 0 if o % 2 == 0 else 4
                if o >= 2:
                    p.wait("tensor", "sVE", vq_evac[o - 2])
                for d in range(8):
                    for h2 in range(2):
                        p.op("tensor", lambda E, o=o, d=d, h2=h2, g0=g0: E.matmul(
                            out=bank(g0 + h2),
                            lhsT=wbfA[:, 1024 * d + 128 * o:1024 * d + 128 * o + 128],
                            rhs=lnT[:, 1024 * d + 512 * h2:1024 * d + 512 * (h2 + 1)],
                            start=(d == 0), stop=False))
                for h2 in range(2):
                    vq = p.op("tensor", lambda E, o=o, h2=h2, g0=g0: E.matmul(
                        out=bank(g0 + h2), lhsT=biasr[0:1, 128 * o:128 * o + 128],
                        rhs=onesb[0:1, 0:512], start=False, stop=True),
                        inc="sPE" if h2 == 1 else None)
                scr = elu_scr[o % 2]
                p.wait("vector", "sPE", vq)
                v1 = p.op("vector", lambda E, scr=scr, g0=g0: E.tensor_single_scalar(
                    out=scr, in_=PS[:, 512 * g0:512 * g0 + 1024], scalar=0.0, op=ALU.min),
                    inc="sVE")
                p.wait("scalar", "sVE", v1)
                v2 = p.op("scalar", lambda E, scr=scr: E.activation(
                    out=scr, in_=scr, func=AF.Exp), inc="sAC")
                p.wait("vector", "sAC", v2)
                vq_evac[o] = p.op("vector", lambda E, o=o, scr=scr, g0=g0: E.scalar_tensor_tensor(
                    out=qh[:, 1024 * o:1024 * (o + 1)], in0=PS[:, 512 * g0:512 * g0 + 1024],
                    scalar=0.0, in1=scr, op0=ALU.max, op1=ALU.add), inc="sVE")
            p.wait("tensor", "sVE", vq_evac[7])

            p.wait("tensor", "sVE", vKVB)
            for pr in range(8):
                for h2 in range(2):
                    vZm = p.op("tensor", lambda E, pr=pr, h2=h2: E.matmul(
                        out=PS[0:16, 512 * (6 + h2):512 * (6 + h2) + 512],
                        lhsT=ksel[:, 16 * pr:16 * (pr + 1)],
                        rhs=qh[:, 1024 * pr + 512 * h2:1024 * pr + 512 * (h2 + 1)],
                        start=(pr == 0), stop=(pr == 7)),
                        inc="sPE" if (pr == 7 and h2 == 1) else None)
            p.wait("vector", "sPE", vZm)
            p.gop("vector", lambda E: E.tensor_scalar(
                out=zr1[:, 0:512], in0=PS[0:16, 3072:3584], scalar1=ZEPS, scalar2=None, op0=ALU.add))
            p.gop("vector", lambda E: E.tensor_scalar(
                out=zr1[:, 512:1024], in0=PS[0:16, 3584:4096], scalar1=ZEPS, scalar2=None, op0=ALU.add))
            p.gop("vector", lambda E: E.reciprocal(out=zr2[:], in_=zr1[:]))
            vZR = p.c["sVE"]

            p.wait("tensor", "sVE", vZR)
            p.wait("tensor", "sSELP", vSELP)
            last_evac = {}
            for pr in range(8):
                ob = 0 if pr % 2 == 0 else 4
                zb_b = 2 if pr % 2 == 0 else 6
                if pr >= 2:
                    p.wait("tensor", "sVE", last_evac[pr - 2])
                for h2 in range(2):
                    p.op("tensor", lambda E, pr=pr, h2=h2, ob=ob: E.matmul(
                        out=bank(ob + h2), lhsT=kvb[:, 128 * pr:128 * (pr + 1)],
                        rhs=qh[:, 1024 * pr + 512 * h2:1024 * pr + 512 * (h2 + 1)],
                        start=True, stop=True))
                    vs2 = p.op("tensor", lambda E, pr=pr, h2=h2, zb_b=zb_b: E.matmul(
                        out=bank(zb_b + h2), lhsT=selp[:, 128 * pr:128 * (pr + 1)],
                        rhs=zr2[:, 512 * h2:512 * (h2 + 1)], start=True, stop=True),
                        inc="sPE" if h2 == 1 else None)
                p.wait("scalar", "sPE", vs2)
                if pr >= 1:
                    p.wait("scalar", "sVE", last_evac[pr - 1])
                va = p.op("scalar", lambda E, zb_b=zb_b: E.copy(
                    out=zbuf, in_=PS[:, 512 * zb_b:512 * zb_b + 1024]), inc="sAC")
                p.wait("vector", "sAC", va)
                last_evac[pr] = p.op("vector", lambda E, pr=pr, ob=ob: E.tensor_tensor(
                    out=lnT[:, 1024 * pr:1024 * (pr + 1)],
                    in0=PS[:, 512 * ob:512 * ob + 1024], in1=zbuf, op=ALU.mult), inc="sVE")
            vATT = p.c["sVE"]

            p.wait("tensor", "sVE", vATT)
            last_x = {}
            for i in range(NTH):
                g0 = 0 if i % 2 == 0 else 4
                if i >= 2:
                    p.wait("tensor", "sVE", last_x[i - 2])
                for d in range(8):
                    for h2 in range(2):
                        p.op("tensor", lambda E, i=i, d=d, h2=h2, g0=g0: E.matmul(
                            out=bank(g0 + h2),
                            lhsT=lnT[:, 1024 * d + 128 * i:1024 * d + 128 * i + 128],
                            rhs=wbfB[:, 8192 + 1024 * d + 512 * h2:8192 + 1024 * d + 512 * (h2 + 1)],
                            start=(d == 0), stop=False))
                for h2 in range(2):
                    vo = p.op("tensor", lambda E, h2=h2, g0=g0: E.matmul(
                        out=bank(g0 + h2), lhsT=onesb[0:1, 0:128],
                        rhs=bo_row[:, 512 * h2:512 * (h2 + 1)],
                        start=False, stop=True), inc="sPE" if h2 == 1 else None)
                p.wait("vector", "sPE", vo)
                last_x[i] = p.op("vector", lambda E, i=i, g0=g0: E.scalar_tensor_tensor(
                    out=xh[:, 1024 * i:1024 * (i + 1)], in0=PS[:, 512 * g0:512 * g0 + 1024],
                    scalar=0.0, in1=xh[:, 1024 * i:1024 * (i + 1)],
                    op0=ALU.add, op1=ALU.add), inc="sVE")
            vXATT = p.c["sVE"]

            dfree = ("sPE", p.c["sPE"])
            mark_hc0 = mark_hc1 = None

            p.wait("vector", "sVE", vXATT)
            for i in range(NTH):
                p.op("vector", lambda E, i=i: E.bn_stats(
                    out=bn12w[:, 12 * i:12 * i + 6], in_=xh[:, 1024 * i:1024 * i + 512]))
                p.op("vector", lambda E, i=i: E.bn_stats(
                    out=bn12w[:, 12 * i + 6:12 * i + 12],
                    in_=xh[:, 1024 * i + 512:1024 * (i + 1)]),
                    inc="sVE" if i == NTH - 1 else None)
            p.wait("vector", "sVE", p.c["sVE"])
            for i in range(NTH):
                p.op("vector", lambda E, i=i: E.bn_aggr(
                    out=st2[:, 2 * i:2 * i + 2], in_=bn12w[:, 12 * i:12 * i + 12]),
                    inc="sVE" if i == NTH - 1 else None)
            p.wait("vector", "sVE", p.c["sVE"])
            sigb = mskb
            nmv = st2.rearrange("p (t two) -> p t two", two=2)
            p.gop("vector", lambda E: E.tensor_scalar(
                out=nmv[:, :, 0:1], in0=nmv[:, :, 0:1], scalar1=-1.0, scalar2=None, op0=ALU.mult))
            vv_ = p.gop("vector", lambda E: E.tensor_scalar(
                out=nmv[:, :, 1:2], in0=nmv[:, :, 1:2], scalar1=LNEPS, scalar2=None, op0=ALU.add))
            p.wait("scalar", "sVE", vv_)
            p.gop("scalar", lambda E: E.activation(
                out=nmv[:, :, 1:2], in_=nmv[:, :, 1:2], func=AF.Ln))
            va_ = p.gop("scalar", lambda E: E.activation(
                out=nmv[:, :, 1:2], in_=nmv[:, :, 1:2], func=AF.Exp, scale=-0.5))
            p.wait("vector", "sAC", va_)
            for i in range(NTH):
                g = NTH * hf + i
                stat = st2[:, 2 * i:2 * i + 2]
                lt = lnts[i % 2]
                sln = "sLNa" if i % 2 == 0 else "sLNb"
                if i >= 2:
                    p.wait("vector", sln, p.c[sln])
                vn = p.op("vector", lambda E, i=i, lt=lt, s=stat: E.tensor_scalar(
                    out=lt, in0=xh[:, 1024 * i:1024 * (i + 1)], scalar1=s[:, 0:1],
                    scalar2=s[:, 1:2], op0=ALU.add, op1=ALU.mult), inc="sVE")
                p.wait("gpsimd", "sVE", vn)
                p.dma("gpsimd", ln2d[128 * g:128 * (g + 1), :], lt, sln)
                p.wait("tensor", "sVE", vn)
                bb = 3 if i % 2 == 0 else 1
                sb_ = 2 if i % 2 == 0 else 0
                for c in range(8):
                    vtp = p.op("tensor", lambda E, c=c, lt=lt, bb=bb: E.transpose(
                        out=bank_bf(bb)[:, 128 * c:128 * (c + 1)],
                        in_=lt[:, 128 * c:128 * (c + 1)], identity=identb[:]),
                        inc="sPE" if c == 7 else None)
                p.wait("vector", "sPE", vtp)
                vevt = p.op("vector", lambda E, i=i, bb=bb: E.tensor_copy(
                    out=lnT.rearrange("p (c ti q) -> p c ti q", c=8, ti=8)[:, :, i, :],
                    in_=bank_bf(bb)[:].rearrange("p (c q) -> p c q", c=8)), inc="sVE")
                p.wait("tensor", "sVE", vevt)
                for d in range(8):
                    vsc = p.op("tensor", lambda E, i=i, d=d, sb_=sb_: E.matmul(
                        out=PS[:, 512 * sb_:512 * sb_ + 8],
                        lhsT=lnT[:, 1024 * d + 128 * i:1024 * d + 128 * i + 128],
                        rhs=routerb[:, 8 * d:8 * (d + 1)],
                        start=(d == 0), stop=(d == 7)), inc="sPE" if d == 7 else None)
                p.wait("vector", "sPE", vsc)
                p.gop("vector", lambda E, i=i, sb_=sb_: E.tensor_copy(
                    out=scall[:, 8 * i:8 * (i + 1)], in_=PS[:, 512 * sb_:512 * sb_ + 8]))
                mk = wcs.drain(6, dst_free=dfree)
                if i == 3:
                    mark_hc0 = mk
                if i == 7:
                    mark_hc1 = mk
            p.wait("vector", "sGF", p.c["sGF"])
            sc3 = scall.rearrange("p (t e) -> p t e", e=8)
            p.gop("vector", lambda E: E.tensor_reduce(
                out=mxb[:, 0:8], in_=sc3, axis=AX.X, op=ALU.max))
            p.gop("vector", lambda E: E.tensor_tensor(
                out=mskb.rearrange("p (t e) -> p t e", e=8), in0=sc3,
                in1=mxb[:, 0:8].rearrange("p (t o) -> p t o", o=1).broadcast_to([128, 8, 8]),
                op=ALU.is_ge))
            p.gop("vector", lambda E: E.scalar_tensor_tensor(
                out=mskb[:], in0=mskb[:], scalar=-1e30, in1=scall[:], op0=ALU.mult, op1=ALU.add))
            p.gop("vector", lambda E: E.tensor_reduce(
                out=mxb[:, 8:16], in_=mskb.rearrange("p (t e) -> p t e", e=8), axis=AX.X, op=ALU.max))
            p.gop("vector", lambda E: E.tensor_tensor(
                out=gatb.rearrange("p (t e) -> p t e", e=8), in0=sc3,
                in1=mxb[:, 8:16].rearrange("p (t o) -> p t o", o=1).broadcast_to([128, 8, 8]),
                op=ALU.is_ge))
            p.gop("vector", lambda E: E.tensor_copy(out=selm, in_=gatb[:]))
            vsel = p.gop("vector", lambda E: E.tensor_copy(out=gatbb[:], in_=selm))
            vg1 = p.c["sVE"]
            p.wait("scalar", "sVE", vg1)
            vsg = p.op("scalar", lambda E: E.activation(out=sigb[:], in_=scall[:], func=AF.Sigmoid),
                       inc="sAC")
            p.wait("vector", "sAC", vsg)
            vmu = p.op("vector", lambda E: E.tensor_tensor(
                out=gatb[:], in0=gatb[:], in1=sigb[:], op=ALU.mult), inc="sVE")
            p.wait("vector", "sVE", vmu)
            vism = p.gop("vector", lambda E: E.tensor_tensor(
                out=mskb.rearrange("p (t e) -> p t e", e=8), in0=sc3,
                in1=mxb[:, 0:8].rearrange("p (t o) -> p t o", o=1).broadcast_to([128, 8, 8]),
                op=ALU.is_ge))

            rt_pe_mark = {}
            base_upd = {0: p.c["sVE"]}
            scat_mark = {}
            if hf == 0:
                p.op("gpsimd", lambda E: bc_cell.__setitem__("r", E.to_reg(NSLOT - 1)) or E.nop())

            def routing_A(i):
                rb = 4 if i % 2 == 0 else 5
                tb = 6 if i % 2 == 0 else 7
                p.wait("tensor", "sVE", base_upd[i])
                p.op("tensor", lambda E, i=i, rb=rb: E.matmul(
                    out=PS[:, 512 * rb:512 * rb + 8], lhsT=tri_sb[:],
                    rhs=gatbb[:, 8 * i:8 * (i + 1)], start=True, stop=False))
                p.op("tensor", lambda E, rb=rb: E.matmul(
                    out=PS[:, 512 * rb:512 * rb + 8], lhsT=onesf[:],
                    rhs=base_row[:], start=False, stop=True))
                rt_pe_mark[i] = p.op("tensor", lambda E, i=i, tb=tb: E.matmul(
                    out=PS[0:1, 512 * tb:512 * tb + 8], lhsT=onescol[:],
                    rhs=gatbb[:, 8 * i:8 * (i + 1)], start=True, stop=True), inc="sPE")

            def routing_B(i):
                g = NTH * hf + i
                rb = 4 if i % 2 == 0 else 5
                tb = 6 if i % 2 == 0 else 7
                ps_r = PS[:, 512 * rb:512 * rb + 8]
                p.wait("vector", "sPE", rt_pe_mark[i])
                p.wait("vector", "sSC", p.c["sSC"])
                p.gop("vector", lambda E, ps_r=ps_r: E.tensor_tensor(
                    out=mxb[:, 8:16], in0=ps_r, in1=thr_b[:], op=ALU.is_ge))
                p.gop("vector", lambda E, i=i: E.tensor_scalar(
                    out=posf[:], in0=selm[:, 8 * i:8 * (i + 1)], scalar1=-1e9, scalar2=1e9,
                    op0=ALU.mult, op1=ALU.add))
                p.gop("vector", lambda E: E.scalar_tensor_tensor(
                    out=posf[:], in0=mxb[:, 8:16], scalar=1e9, in1=posf[:],
                    op0=ALU.mult, op1=ALU.add))
                p.gop("vector", lambda E, ps_r=ps_r: E.tensor_tensor(
                    out=posf[:], in0=posf[:], in1=ps_r, op=ALU.add))
                vpos = p.gop("vector", lambda E: E.tensor_copy(out=posi[:], in_=posf[:]))
                vbase = p.gop("vector", lambda E, tb=tb: E.tensor_tensor(
                    out=base_row[:], in0=base_row[:], in1=PS[0:1, 512 * tb:512 * tb + 8],
                    op=ALU.add))
                base_upd[i + 1] = vbase
                tmp = bn12w[:, 64:72]
                p.gop("vector", lambda E, i=i, tmp=tmp: E.tensor_tensor(
                    out=tmp, in0=mskb[:, 8 * i:8 * (i + 1)], in1=posf[:], op=ALU.mult))
                p.gop("vector", lambda E, tmp=tmp: E.tensor_reduce(
                    out=invf[:, 0:1], in_=tmp, axis=AX.X, op=ALU.add))
                p.gop("vector", lambda E, i=i, tmp=tmp: E.tensor_tensor(
                    out=tmp, in0=selm[:, 8 * i:8 * (i + 1)], in1=mskb[:, 8 * i:8 * (i + 1)],
                    op=ALU.subtract))
                p.gop("vector", lambda E, tmp=tmp: E.tensor_tensor(
                    out=tmp, in0=tmp, in1=posf[:], op=ALU.mult))
                p.gop("vector", lambda E, tmp=tmp: E.tensor_reduce(
                    out=invf[:, 1:2], in_=tmp, axis=AX.X, op=ALU.add))
                p.gop("vector", lambda E: E.tensor_single_scalar(
                    out=invf[:], in_=invf[:], scalar=float(NSLOT), op=ALU.min))
                vinv = p.op("vector", lambda E, g=g: E.tensor_copy(
                    out=invs[:, 2 * g:2 * g + 2], in_=invf[:]), inc="sVE")
                p.wait("gpsimd", "sVE", vinv)
                for e_ in range(NE):
                    p.op("gpsimd", lambda E, e_=e_, g=g: E.indirect_dma_start(
                        out=idxd.rearrange("(a b) -> a b", b=1),
                        out_offset=bass.IndirectOffsetOnAxis(ap=posi[:, e_:e_ + 1], axis=0),
                        in_=tok_sb[:, g:g + 1], in_offset=None,
                        bounds_check=bc_cell["r"], oob_is_err=False), inc="sSC", n=16)
                p.dma("gpsimd", gatd[128 * g:128 * (g + 1), :], gatb[:, 8 * i:8 * (i + 1)], "sGF")
                scat_mark[i] = p.c["sSC"]

            p.wait("vector", "sLNa", p.c["sLNa"])
            p.wait("vector", "sLNb", p.c["sLNb"])
            for tq in range(2):
                last_h = {}
                f15_mark = None
                for f in range(16):
                    mb = 0 if f % 2 == 0 else 2
                    sb_ = 1 if f % 2 == 0 else 3
                    if tq == 0 and f == 0:
                        p.wait_marks("tensor", mark_hc0)
                    if tq == 0 and f == 8:
                        p.wait_marks("tensor", mark_hc1)
                    if tq == 0 and f % 2 == 0:
                        routing_A(f // 2)
                    if tq == 0 and f % 2 == 1:
                        routing_B(f // 2)
                    if f >= 2:
                        p.wait("tensor", "sVE", last_h[f - 2])
                    for d in range(8):
                        rhs_ap = lnT[:, 1024 * d + 512 * tq:1024 * d + 512 * tq + 512]
                        p.op("tensor", lambda E, f=f, d=d, mb=mb, r=rhs_ap: E.matmul(
                            out=bank(mb),
                            lhsT=wbfA[:, 2048 * d + 128 * f:2048 * d + 128 * f + 128],
                            rhs=r, start=(d == 0), stop=(d == 7)))
                    for d in range(8):
                        rhs_ap = lnT[:, 1024 * d + 512 * tq:1024 * d + 512 * tq + 512]
                        vmm = p.op("tensor", lambda E, f=f, d=d, sb_=sb_, r=rhs_ap: E.matmul(
                            out=bank(sb_),
                            lhsT=wbfB[:, 2048 * d + 128 * f:2048 * d + 128 * f + 128],
                            rhs=r, start=(d == 0), stop=(d == 7)),
                            inc="sPE" if d == 7 else None)
                    if f == 15:
                        f15_mark = vmm
                    p.wait("scalar", "sPE", vmm)
                    vsi = p.op("scalar", lambda E, f=f, mb=mb: E.activation(
                        out=m2s[f % 2], in_=bank(mb), func=AF.Silu), inc="sAC")
                    p.wait("vector", "sAC", vsi)
                    last_h[f] = p.op("vector", lambda E, f=f, sb_=sb_: E.tensor_tensor(
                        out=qh[:, 512 * f:512 * (f + 1)], in0=m2s[f % 2], in1=bank(sb_),
                        op=ALU.mult), inc="sVE")
                vH = p.c["sVE"]

                p.wait("tensor", "sVE", vH)
                last_xo = {}
                for j in range(4):
                    i = 4 * tq + j
                    g0 = 4 if j % 2 == 0 else 6
                    if j >= 2:
                        p.wait("tensor", "sVE", last_xo[j - 2])
                    if hf == 1 and tq == 1 and wcs.pending:
                        mark_hc0 = wcs.drain(4, dst_free=("sPE", f15_mark))
                    for f in range(16):
                        for h2 in range(2):
                            vom = p.op("tensor", lambda E, f=f, j=j, h2=h2, g0=g0: E.matmul(
                                out=bank(g0 + h2),
                                lhsT=qh[:, 512 * f + 128 * j:512 * f + 128 * (j + 1)],
                                rhs=wod[:, 1024 * f + 512 * h2:1024 * f + 512 * (h2 + 1)],
                                start=(f == 0), stop=(f == 15)),
                                inc="sPE" if (f == 15 and h2 == 1) else None)
                    p.wait("vector", "sPE", vom)
                    last_xo[j] = p.op("vector", lambda E, i=i, g0=g0: E.scalar_tensor_tensor(
                        out=xh[:, 1024 * i:1024 * (i + 1)], in0=PS[:, 512 * g0:512 * g0 + 1024],
                        scalar=0.0, in1=xh[:, 1024 * i:1024 * (i + 1)],
                        op0=ALU.add, op1=ALU.add), inc="sVE")
            p.wait("gpsimd", "sVE", p.c["sVE"])
            for i in range(NTH):
                t = toff + i
                p.dma("gpsimd", xmid[128 * t:128 * (t + 1), :], xh[:, 1024 * i:1024 * (i + 1)],
                      "sST")
            if hf == 0:
                p.wait("sync", "sST", p.c["sST"])

        sh_ph2_done = ("sPE", p.c["sPE"])
        p.wait("sync", "sSC", p.c["sSC"])
        for col in range(40):
            e_, j_ = divmod(col, 5)
            p.dma("sync", idxs[:, col:col + 1],
                  idxd[CAP * e_ + 128 * j_:CAP * e_ + 128 * (j_ + 1)].rearrange(
                      "(a b) -> a b", b=1), "sIX")
        vIX = p.c["sIX"]
        load_expert_insw(0)
        load_expert_wout(0)

        pk_raw = [scr8.bitcast(BF16)[:, 1024 * s:1024 * (s + 1)] for s in range(2)]
        xh_bf = xh.bitcast(BF16)
        pk_sets = [[xh_bf[:, 4096 + 5120 * st + 1024 * s:4096 + 5120 * st + 1024 * (s + 1)]
                    for s in range(5)] for st in range(2)]
        m2sp = [m2[:, 0:640], gbuf[:, 0:640]]
        pout = [xh[:, 0:1024], xh[:, 1024:2048]]

        def h_ap(f, c0, c1):
            if f < 12:
                return lnT[:, 640 * f + c0:640 * f + c1]
            return qh[:, 5120 + 640 * (f - 12) + c0:5120 + 640 * (f - 12) + c1]

        def rt_ap(d, c0, c1):
            return qh[:, 640 * d + c0:640 * d + c1]

        gfull = p.c["sGF"]
        pk_gate = {}
        tr_mark_set = {}

        def gather_scale(e, first):
            pk_sc = pk_sets[e % 2]
            p.wait("gpsimd", "sIX", vIX)
            p.wait("gpsimd", "sGF", gfull)
            if first:
                p.wait("vector", "sST", p.c["sST"])
            for j in range(5):
                col = 5 * e + j
                sga = "sGA" if j % 2 == 0 else "sGB"
                sgc = "sGC" if j % 2 == 0 else "sGD"
                if pk_gate.get(j % 2):
                    p.wait("gpsimd", "sVE", pk_gate[j % 2])
                vg1_ = p.op("gpsimd", lambda E, j=j, col=col: E.indirect_dma_start(
                    out=pk_raw[j % 2], out_offset=None,
                    in_=ln2d[:], in_offset=bass.IndirectOffsetOnAxis(
                        ap=idxs[:, col:col + 1], axis=0)), inc=sga, n=16)
                vg2_ = p.op("gpsimd", lambda E, j=j, col=col: E.indirect_dma_start(
                    out=mxb[:, 8 * (j % 2):8 * (j % 2) + 8], out_offset=None,
                    in_=gatd[:], in_offset=bass.IndirectOffsetOnAxis(
                        ap=idxs[:, col:col + 1], axis=0)), inc=sgc, n=16)
                p.wait("vector", sga, vg1_)
                p.wait("vector", sgc, vg2_)
                if tr_mark_set.get(e % 2):
                    p.wait("vector", "sPE", tr_mark_set[e % 2])
                pk_gate[j % 2] = p.op("vector", lambda E, j=j, e=e, pk_sc=pk_sc: E.tensor_scalar(
                    out=pk_sc[j], in0=pk_raw[j % 2],
                    scalar1=mxb[:, 8 * (j % 2) + e:8 * (j % 2) + e + 1], scalar2=None,
                    op0=ALU.mult), inc="sVE")
            return p.c["sVE"]

        vsc_marks = {0: gather_scale(0, True), 1: gather_scale(1, False)}

        pout_slot_mark = {0: 0, 1: 0}
        pout_marks = {}
        f15_prev = None
        for e in range(NE):
            sh_prev = ("sPE", p.c["sPE"])
            if e == 0:
                mark_hc0 = wcs.drain(16, dst_free=sh_ph2_done)
                mark_hc1 = wcs.drain(16, dst_free=sh_ph2_done)
                wod_cur = wcs.drain(16, dst_free=sh_ph2_done)
            else:
                mark_hc1 = wcs.drain(12, dst_free=("sPE", f15_prev))
                wod_cur = wcs.drain(16, dst_free=sh_prev)
            pk_sc = pk_sets[e % 2]
            p.wait("tensor", "sVE", vsc_marks[e])
            for j in range(5):
                if j >= 1:
                    p.wait("tensor", "sVE", trev)
                for c in range(8):
                    vtp = p.op("tensor", lambda E, j=j, c=c, pk_sc=pk_sc: E.transpose(
                        out=bank_bf(0)[:, 128 * c:128 * (c + 1)],
                        in_=pk_sc[j][:, 128 * c:128 * (c + 1)], identity=identb[:]),
                        inc="sPE" if c == 7 else None)
                p.wait("vector", "sPE", vtp)
                trev = p.op("vector", lambda E, j=j: E.tensor_copy(
                    out=qh[:, 0:5120].rearrange("p (d c) -> p d c", c=640)[:, :, 128 * j:128 * (j + 1)],
                    in_=bank_bf(0)[:].rearrange("p (d c) -> p d c", c=128)), inc="sVE")
            tr_mark_set[e % 2] = p.c["sPE"]
            vrt = p.c["sVE"]
            if e + 1 < NE:
                load_expert_insw(e + 1)
                load_expert_wout(e + 1)

            p.wait("tensor", "sVE", vrt)
            last_h = {}
            f15_mark = None
            for f in range(16):
                base_c = 0 if f % 2 == 0 else 2048
                sw_c = 1024 if f % 2 == 0 else 3072
                if f == 0:
                    p.wait_marks("tensor", mark_hc0)
                if f == 8:
                    p.wait_marks("tensor", mark_hc1)
                if f >= 2:
                    p.wait("tensor", "sVE", last_h[f - 2])
                for mat, cbase in ((wbfA, base_c), (wbfB, sw_c)):
                    for d in range(8):
                        p.op("tensor", lambda E, f=f, d=d, mat=mat, cb=cbase: E.matmul(
                            out=PS[:, cb:cb + 512],
                            lhsT=mat[:, 2048 * d + 128 * f:2048 * d + 128 * f + 128],
                            rhs=rt_ap(d, 0, 512), start=(d == 0), stop=(d == 7)))
                    for d in range(8):
                        vmm = p.op("tensor", lambda E, f=f, d=d, mat=mat, cb=cbase: E.matmul(
                            out=PS[:, cb + 512:cb + 640],
                            lhsT=mat[:, 2048 * d + 128 * f:2048 * d + 128 * f + 128],
                            rhs=rt_ap(d, 512, 640), start=(d == 0), stop=(d == 7)),
                            inc="sPE" if (d == 7 and mat is wbfB) else None)
                if f == 15:
                    f15_mark = vmm
                p.wait("scalar", "sPE", vmm)
                vsi = p.op("scalar", lambda E, f=f, cb=base_c: E.activation(
                    out=m2sp[f % 2], in_=PS[:, cb:cb + 640], func=AF.Silu), inc="sAC")
                p.wait("vector", "sAC", vsi)
                last_h[f] = p.op("vector", lambda E, f=f, sc=sw_c: E.tensor_tensor(
                    out=h_ap(f, 0, 640), in0=m2sp[f % 2], in1=PS[:, sc:sc + 640],
                    op=ALU.mult), inc="sVE")
            vH = p.c["sVE"]
            if e + 2 < NE:
                vsc_marks[e + 2] = gather_scale(e + 2, False)

            f15_prev = f15_mark
            p.wait("tensor", "sVE", vH)
            p.wait_marks("tensor", wod_cur)
            last_po = {}
            for j in range(5):
                g0 = 4 if j % 2 == 0 else 6
                if j >= 2:
                    p.wait("tensor", "sVE", last_po[j - 2])
                p.wait("vector", "sPa" if j % 2 == 0 else "sPb", pout_slot_mark[j % 2])
                if e + 1 < NE and wcs.pending:
                    mark_hc0 = wcs.drain(4, dst_free=("sPE", f15_mark))
                for f in range(16):
                    for h2 in range(2):
                        vom = p.op("tensor", lambda E, f=f, j=j, h2=h2, g0=g0: E.matmul(
                            out=bank(g0 + h2),
                            lhsT=h_ap(f, 128 * j, 128 * (j + 1)),
                            rhs=wod[:, 1024 * f + 512 * h2:1024 * f + 512 * (h2 + 1)],
                            start=(f == 0), stop=(f == 15)),
                            inc="sPE" if (f == 15 and h2 == 1) else None)
                p.wait("vector", "sPE", vom)
                last_po[j] = p.op("vector", lambda E, j=j, g0=g0: E.tensor_copy(
                    out=pout[j % 2], in_=PS[:, 512 * g0:512 * g0 + 1024]), inc="sVE")
                p.wait("gpsimd", "sVE", last_po[j])
                sp_ = "sPa" if j % 2 == 0 else "sPb"
                p.dma("gpsimd", poutd[CAP * e + 128 * j:CAP * e + 128 * (j + 1), :],
                      pout[j % 2], sp_)
                pout_slot_mark[j % 2] = p.c[sp_]
        vPa, vPb = p.c["sPa"], p.c["sPb"]

        ob1 = [xh[:, 2048 + 1024 * s:2048 + 1024 * (s + 1)] for s in range(2)]
        ob2 = [xh[:, 4096 + 1024 * s:4096 + 1024 * (s + 1)] for s in range(2)]
        xb = [xh[:, 6144 + 1024 * s:6144 + 1024 * (s + 1)] for s in range(2)]
        wb_mark = {}
        st_mark = {}
        p.wait("sync", "sPE", tr_mark_set[1])
        for g in range(16):
            sl = g % 2
            if g >= 2:
                p.wait("sync", "sPa" if g % 2 == 0 else "sPb", st_mark[g - 2])
                p.wait("gpsimd", "sVE", wb_mark[g - 2])
            sx = "sLXa" if g % 2 == 0 else "sLXb"
            p.dma("sync", xb[sl], xmid[128 * g:128 * (g + 1), :], sx)
            vxl = p.c[sx]
            if g == 0:
                p.wait("gpsimd", "sPa", vPa)
                p.wait("gpsimd", "sPb", vPb)
            sga = "sGA" if g % 2 == 0 else "sGB"
            sgc = "sGC" if g % 2 == 0 else "sGD"
            v1_ = p.op("gpsimd", lambda E, g=g, sl=sl: E.indirect_dma_start(
                out=ob1[sl], out_offset=None,
                in_=poutd[:], in_offset=bass.IndirectOffsetOnAxis(
                    ap=invs[:, 2 * g:2 * g + 1], axis=0)), inc=sga, n=16)
            v2_ = p.op("gpsimd", lambda E, g=g, sl=sl: E.indirect_dma_start(
                out=ob2[sl], out_offset=None,
                in_=poutd[:], in_offset=bass.IndirectOffsetOnAxis(
                    ap=invs[:, 2 * g + 1:2 * g + 2], axis=0)), inc=sgc, n=16)
            p.wait("vector", sga, v1_)
            p.wait("vector", sgc, v2_)
            p.wait("vector", sx, vxl)
            va1 = p.op("vector", lambda E, sl=sl: E.tensor_tensor(
                out=xb[sl], in0=xb[sl], in1=ob1[sl], op=ALU.add), inc="sVE")
            p.wait("vector", "sVE", va1)
            vwb = p.op("vector", lambda E, sl=sl: E.tensor_tensor(
                out=xb[sl], in0=xb[sl], in1=ob2[sl], op=ALU.add), inc="sVE")
            wb_mark[g] = vwb
            p.wait("gpsimd", "sVE", vwb)
            spq = "sPa" if g % 2 == 0 else "sPb"
            p.dma("gpsimd", y_out[128 * g:128 * (g + 1), :], xb[sl], spq)
            st_mark[g] = p.c[spq]
        p.wait("gpsimd", "sPa", p.c["sPa"])
        p.wait("gpsimd", "sPb", p.c["sPb"])

        with nc.Block() as block:
            @block.sync
            def _(E):
                for f in p.q["sync"]:
                    f(E)

            @block.tensor
            def _(E):
                for f in p.q["tensor"]:
                    f(E)

            @block.vector
            def _(E):
                for f in p.q["vector"]:
                    f(E)

            @block.scalar
            def _(E):
                for f in p.q["scalar"]:
                    f(E)

            @block.gpsimd
            def _(E):
                for f in p.q["gpsimd"]:
                    f(E)

    return nc


_CACHE = {}


def kernel(**inp):
    x = np.ascontiguousarray(np.asarray(inp["L0_attn_x"], dtype=np.float32))
    shared = {
        "wqT": np.ascontiguousarray(np.asarray(inp["L0_attn_Wq"], dtype=np.float32).T),
        "wkT": np.ascontiguousarray(np.asarray(inp["L0_attn_Wk"], dtype=np.float32).T),
        "wvT": np.ascontiguousarray(np.asarray(inp["L0_attn_Wv"], dtype=np.float32).T),
        "woT": np.ascontiguousarray(np.asarray(inp["L0_attn_Wo"], dtype=np.float32).T),
        "cbias": np.ascontiguousarray(np.stack([
            np.asarray(inp["L0_attn_bq"]), np.asarray(inp["L0_attn_bk"]),
            np.asarray(inp["L0_attn_bv"]), np.asarray(inp["L0_attn_bo"])]).astype(np.float32)),
        "router": np.ascontiguousarray(np.asarray(inp["L0_ffn_router_DE"], dtype=np.float32)),
        "w_in_e": np.ascontiguousarray(np.asarray(inp["L0_ffn_w_in_eDF"], dtype=np.float32)),
        "w_sw_e": np.ascontiguousarray(np.asarray(inp["L0_ffn_w_swiglu_eDF"], dtype=np.float32)),
        "w_out_e": np.ascontiguousarray(np.asarray(inp["L0_ffn_w_out_eFD"], dtype=np.float32)),
        "w_in_s": np.ascontiguousarray(np.asarray(inp["L0_ffn_w_in_shared_FD"], dtype=np.float32).T),
        "w_sw_s": np.ascontiguousarray(np.asarray(inp["L0_ffn_w_swiglu_FD"], dtype=np.float32).T),
        "w_out_s": np.ascontiguousarray(np.asarray(inp["L0_ffn_w_out_shared_DF"], dtype=np.float32).T),
        "c_id_bf": np.eye(128, dtype=ml_dtypes.bfloat16),
        "c_ones_bf": np.ones((1, 512), dtype=ml_dtypes.bfloat16),
        "c_tri": np.triu(np.ones((128, 128), dtype=np.float32), 1).T.astype(ml_dtypes.bfloat16),
        "c_onesf": np.ones((1, 128), dtype=np.float32),
        "c_tok": (np.arange(16, dtype=np.int32)[None, :] * 128
                  + np.arange(128, dtype=np.int32)[:, None]).astype(np.int32),
        "c_thr": ((np.arange(8, dtype=np.float32) + 1.0) * 640.0)[None, :],
        "c_base": (np.arange(8, dtype=np.float32) * 640.0)[None, :],
        "c_zidx": np.zeros((1, 8 * 640), dtype=np.int32),
    }
    selp = np.zeros((16, 1024), dtype=np.float32)
    for pr in range(8):
        for j in range(128):
            selp[2 * pr + (j // 64), 128 * pr + j] = 1.0
    shared["c_selp"] = selp

    if "nc" not in _CACHE:
        _CACHE["nc"] = build_program()
    nc = _CACHE["nc"]
    in_maps = [dict(x=np.ascontiguousarray(x[c]), **shared) for c in range(8)]
    res = run_bass_kernel_spmd(nc, in_maps, core_ids=list(range(8)))
    return np.stack([np.asarray(r["y"], dtype=np.float32) for r in res.results], axis=0)


# version 15
# speedup vs baseline: 1.3182x; 1.3182x over previous
import contextlib
from collections import defaultdict

import numpy as np
import ml_dtypes

import concourse.bass as bass
import concourse.mybir as mybir
from concourse.bass_utils import run_bass_kernel_spmd

F32 = mybir.dt.float32
BF16 = mybir.dt.bfloat16
AF = mybir.ActivationFunctionType
ALU = mybir.AluOpType
AX = mybir.AxisListType

T, D, FF, NE = 2048, 1024, 2048, 8
NT, NTH, NPAIR = 16, 8, 8
CAP = 640
NSLOT = NE * CAP
LNEPS, ZEPS = 1e-5, 1e-6
I32 = mybir.dt.int32

ENG_SEM = {"tensor": "sPE", "vector": "sVE", "scalar": "sAC", "gpsimd": "sPO"}


class P:
    def __init__(self, nc):
        self.nc = nc
        self.q = {k: [] for k in ("sync", "tensor", "vector", "scalar", "gpsimd")}
        self.sem = {}
        self.c = defaultdict(int)

    def mksem(self, ctx, *names):
        for n in names:
            self.sem[n] = ctx.enter_context(self.nc.semaphore(n))

    def wait(self, eng, sem, val=None):
        v = self.c[sem] if val is None else val
        if v <= 0:
            return
        h = self.sem[sem]
        self.q[eng].append(lambda E: E.wait_ge(h, v))

    def op(self, eng, fn, inc=None, n=1):
        if inc is None:
            self.q[eng].append(fn)
            return None
        self.c[inc] += n
        h = self.sem[inc]
        self.q[eng].append(lambda E: fn(E).then_inc(h, n))
        return self.c[inc]

    def gop(self, eng, fn):
        v = self.op(eng, fn, inc=ENG_SEM[eng])
        self.wait(eng, ENG_SEM[eng], v)
        return v

    def wait_marks(self, eng, marks):
        for sem, val in marks:
            self.wait(eng, sem, val)

    def dma(self, eng, out, in_, sem):
        return self.op(eng, lambda E: E.dma_start(out=out, in_=in_), inc=sem, n=16)


class CastStream:

    def __init__(self, p, slots, slot_sems, cast_sems, engines):
        self.p, self.slots, self.slot_sems, self.engines = p, slots, slot_sems, engines
        self.cast_sems = cast_sems
        self.k = 0
        self.kc = 0
        self.pending = []

    def load(self, src_ap, dst_ap):
        s = self.k % len(self.slots)
        if self.k >= len(self.slots):
            self.p.wait("sync", self.cast_sems[s], self.k // len(self.slots))
        self.p.dma("sync", self.slots[s], src_ap, self.slot_sems[s])
        self.pending.append((self.k, dst_ap, self.p.c[self.slot_sems[s]]))
        self.k += 1

    def drain(self, count=None, dst_free=None):
        p = self.p
        todo = self.pending if count is None else self.pending[:count]
        self.pending = self.pending[len(todo):]
        for idx, dst, dval in todo:
            eng = self.engines[self.kc % len(self.engines)]
            s = idx % len(self.slots)
            p.wait(eng, self.slot_sems[s], dval)
            if dst_free is not None:
                p.wait(eng, dst_free[0], dst_free[1])
            src = self.slots[s]
            if eng == "scalar":
                p.op(eng, lambda E, d=dst, sp=src: E.copy(out=d, in_=sp),
                     inc=self.cast_sems[s])
            else:
                p.op(eng, lambda E, d=dst, sp=src: E.tensor_copy(out=d, in_=sp),
                     inc=self.cast_sems[s])
            self.kc += 1
        return [(cs, p.c[cs]) for cs in self.cast_sems]


def build_program():
    nc = bass.Bass("TRN2", target_bir_lowering=False)

    def din(name, shape, dt=F32):
        return nc.dram_tensor(name, shape, dt, kind="ExternalInput").ap()

    x_in = din("x", [T, D])
    y_out = nc.dram_tensor("y", [T, D], F32, kind="ExternalOutput").ap()
    wqT, wkT, wvT, woT = (din(n, [D, D]) for n in ("wqT", "wkT", "wvT", "woT"))
    cbias = din("cbias", [4, D])
    router = din("router", [D, NE])
    w_in_e, w_sw_e = din("w_in_e", [NE, D, FF]), din("w_sw_e", [NE, D, FF])
    w_out_e = din("w_out_e", [NE, FF, D])
    w_in_s, w_sw_s = din("w_in_s", [D, FF]), din("w_sw_s", [D, FF])
    w_out_s = din("w_out_s", [FF, D])
    c_id_bf = din("c_id_bf", [128, 128], BF16)
    c_ones_bf = din("c_ones_bf", [1, 512], BF16)
    c_selp = din("c_selp", [16, 1024], F32)
    c_tri = din("c_tri", [128, 128], BF16)
    c_onesf = din("c_onesf", [1, 128], F32)
    c_tok = din("c_tok", [128, 16], I32)
    c_thr = din("c_thr", [1, 8], F32)
    c_base = din("c_base", [1, 8], F32)
    c_zidx = din("c_zidx", [1, NSLOT], I32)
    ln2d = nc.dram_tensor("ln2d", [T, D], BF16, kind="Internal").ap()
    gatd = nc.dram_tensor("gatd", [T, NE], F32, kind="Internal").ap()
    idxd = nc.dram_tensor("idxd", [NSLOT], I32, kind="Internal").ap()
    poutd = nc.dram_tensor("poutd", [NSLOT + 1, D], F32, kind="Internal").ap()
    xmid = nc.dram_tensor("xmid", [T, D], F32, kind="Internal").ap()

    with contextlib.ExitStack() as ctx:
        E_ = ctx.enter_context
        xh = E_(nc.sbuf_tensor("xh", [128, 8192], F32))
        lnT = E_(nc.sbuf_tensor("lnT", [128, 8192], BF16))
        qh = E_(nc.sbuf_tensor("qh", [128, 8192], BF16))
        wbfA = E_(nc.sbuf_tensor("wbfA", [128, 16384], BF16))
        wbfB = E_(nc.sbuf_tensor("wbfB", [128, 16384], BF16))
        wod = E_(nc.sbuf_tensor("wod", [128, 16384], BF16))
        landA = E_(nc.sbuf_tensor("landA", [128, 4096], F32))
        gbuf = E_(nc.sbuf_tensor("gbuf", [128, 1024], F32))
        m2 = E_(nc.sbuf_tensor("m2", [128, 1024], F32))
        scr8 = E_(nc.sbuf_tensor("scr8", [128, 2048], F32))
        kvb = E_(nc.sbuf_tensor("kvb", [128, 1024], BF16))
        ksel = E_(nc.sbuf_tensor("ksel", [128, 128], BF16))
        biasr = E_(nc.sbuf_tensor("biasr", [65, 1024], BF16))
        onesb = E_(nc.sbuf_tensor("onesb", [65, 512], BF16))
        identb = E_(nc.sbuf_tensor("identb", [128, 128], BF16))
        routerf = E_(nc.sbuf_tensor("routerf", [128, 64], F32))
        routerb = E_(nc.sbuf_tensor("routerb", [128, 64], BF16))
        bo_row = E_(nc.sbuf_tensor("bo_row", [1, 1024], BF16))
        tri_sb = E_(nc.sbuf_tensor("tri_sb", [128, 128], BF16))
        onesf = E_(nc.sbuf_tensor("onesf", [1, 128], F32))
        tok_sb = E_(nc.sbuf_tensor("tok_sb", [128, 16], I32))
        onescol = E_(nc.sbuf_tensor("onescol", [128, 1], BF16))
        thr_b = E_(nc.sbuf_tensor("thr_b", [128, 8], F32))
        base_row = E_(nc.sbuf_tensor("base_row", [1, 8], F32))
        posf = E_(nc.sbuf_tensor("posf", [128, 8], F32))
        posi = E_(nc.sbuf_tensor("posi", [128, 8], I32))
        idxs = E_(nc.sbuf_tensor("idxs", [128, 40], I32))
        invs = E_(nc.sbuf_tensor("invs", [128, 32], I32))
        invf = E_(nc.sbuf_tensor("invf", [128, 2], F32))
        st1 = E_(nc.sbuf_tensor("st1", [128, 32], F32))
        st2 = E_(nc.sbuf_tensor("st2", [128, 16], F32))
        bn12w = E_(nc.sbuf_tensor("bn12w", [128, 120], F32))
        bn12 = bn12w[:, 0:12]
        psev = bn12w[:, 96:104]
        psev_tot = bn12w[0:1, 104:112]
        rtmp = bn12w[:, 112:120]
        mvv = E_(nc.sbuf_tensor("mvv", [128, 8], F32))
        scall = E_(nc.sbuf_tensor("scall", [128, 64], F32))
        mskb = E_(nc.sbuf_tensor("mskb", [128, 64], F32))
        mxb = E_(nc.sbuf_tensor("mxb", [128, 16], F32))
        gatb = E_(nc.sbuf_tensor("gatb", [128, 64], F32))
        gatbb = E_(nc.sbuf_tensor("gatbb", [128, 64], BF16))
        rtw = E_(nc.sbuf_tensor("rtw", [1, 256], F32))
        posa = E_(nc.sbuf_tensor("posa", [128, 64], F32))
        posia = E_(nc.sbuf_tensor("posia", [128, 64], I32))
        PS = E_(nc.psum_tensor("PS", [128, 4096], F32))

        bc_cell = {}
        p = P(nc)
        p.mksem(ctx, "sLXa", "sLXb", "sW0", "sW1", "sW2", "sW3",
                "sC0", "sC1", "sC2", "sC3", "sLC", "sSELP", "sGF", "sIX", "sST",
                "sLNa", "sLNb", "sSC", "sGA", "sGB", "sGC", "sGD", "sPa", "sPb",
                "sPE", "sVE", "sAC", "sPO")

        def bank(b):
            return PS[:, 512 * b:512 * (b + 1)]

        def bank_bf(b):
            return PS[:, 512 * b:512 * (b + 1)].bitcast(BF16)

        elu_scr = [scr8[:, 0:1024], scr8[:, 1024:2048]]
        zbuf = scr8[:, 1024:2048]
        rt = scr8.bitcast(BF16)
        zr1 = m2[0:16, :]
        zr2 = scr8[0:16, 0:1024]
        selp = gbuf[0:16, :]
        gslot = [gbuf.bitcast(BF16)[:, 0:512], gbuf.bitcast(BF16)[:, 512:1024]]
        m2s = [m2[:, 0:512], m2[:, 512:1024]]
        kxs = [qh[:, 0:1024], qh[:, 1024:2048]]
        vxs = [qh[:, 2048:3088], qh[:, 3088:4128]]
        aT = qh[:, 4160:5184]
        xa32 = lnT.bitcast(F32)
        lnts = [qh[:, 5184:6208], qh[:, 6208:7232]]
        cb32 = landA[0:65, 0:1024]
        cbo = landA[0:1, 1024:2048]
        selm = bn12w[:, 0:64]
        kv_bank = [0, 0, 0, 1, 1, 1, 2, 2]
        kv_col = [0, 130, 260, 0, 130, 260, 0, 130]

        wcs = CastStream(p, [landA[:, 1024 * s:1024 * (s + 1)] for s in range(4)],
                         ["sW0", "sW1", "sW2", "sW3"], ["sC0", "sC1", "sC2", "sC3"],
                         ["vector", "scalar"])
        ocs = wcs

        p.dma("gpsimd", identb[:], c_id_bf[:], "sLC")
        p.dma("gpsimd", onesb[0:1, :], c_ones_bf[:], "sLC")
        p.dma("gpsimd", onesb[32:33, :], c_ones_bf[:], "sLC")
        p.dma("gpsimd", onesb[64:65, :], c_ones_bf[:], "sLC")
        p.dma("gpsimd", cb32[0:1, :], cbias[0:1, :], "sLC")
        p.dma("gpsimd", cb32[32:33, :], cbias[1:2, :], "sLC")
        p.dma("gpsimd", cb32[64:65, :], cbias[2:3, :], "sLC")
        p.dma("gpsimd", cbo[:], cbias[3:4, :], "sLC")
        for c in range(8):
            p.dma("gpsimd", routerf[:, 8 * c:8 * c + 8], router[128 * c:128 * (c + 1), :], "sLC")
        p.dma("gpsimd", tri_sb[:], c_tri[:], "sLC")
        p.dma("gpsimd", onesf[:], c_onesf[:], "sLC")
        p.dma("gpsimd", tok_sb[:], c_tok[:], "sLC")
        p.dma("gpsimd", onescol[:], c_ones_bf[0:1, 0:128].rearrange("o p -> p o"), "sLC")
        p.dma("gpsimd", thr_b[0:1, :], c_thr[:], "sLC")
        p.dma("gpsimd", base_row[:], c_base[:], "sLC")
        p.dma("gpsimd", idxd.rearrange("(a b) -> a b", a=1), c_zidx[:], "sLC")
        p.dma("gpsimd", poutd[NSLOT:NSLOT + 1, :], c_zidx[0:1, 0:1024].bitcast(F32), "sLC")
        vLC = p.c["sLC"]
        p.wait("vector", "sLC", vLC)
        p.op("vector", lambda E: E.tensor_copy(out=biasr[0:1, 0:1024], in_=cb32[0:1, :]), inc="sVE")
        p.op("vector", lambda E: E.tensor_copy(out=biasr[32:33, 0:1024], in_=cb32[32:33, :]), inc="sVE")
        p.op("vector", lambda E: E.tensor_copy(out=biasr[64:65, 0:1024], in_=cb32[64:65, :]), inc="sVE")
        p.op("vector", lambda E: E.tensor_copy(out=bo_row[:], in_=cbo), inc="sVE")
        vBIAS = p.op("vector", lambda E: E.tensor_copy(out=routerb[:], in_=routerf[:]), inc="sVE")
        p.wait("tensor", "sLC", vLC)
        vthr = p.op("tensor", lambda E: E.matmul(
            out=PS[:, 0:8], lhsT=onesf[0:1, :], rhs=thr_b[0:1, :],
            start=True, stop=True), inc="sPE")
        p.wait("vector", "sPE", vthr)
        vthr2 = p.op("vector", lambda E: E.tensor_copy(out=thr_b[:], in_=PS[:, 0:8]), inc="sVE")
        p.op("gpsimd", lambda E: E.memset(kvb[:], 0.0), inc="sPO")
        p.op("gpsimd", lambda E: E.memset(ksel[:], 0.0), inc="sPO")
        for s in range(2):
            p.op("gpsimd",
                 lambda E, s=s: E.memset(vxs[s].rearrange("p (g c) -> p g c", c=65)[:, :, 64:65], 1.0),
                 inc="sPO")
        vMEMS = p.c["sPO"]

        def load_attn_weights(full):
            mats = ((wkT, wbfA[:, 8192:16384]), (wvT, wbfB[:, 0:8192])) if full else ()
            mats = mats + ((wqT, wbfA[:, 0:8192]), (woT, wbfB[:, 8192:16384]))
            for src, dst in mats:
                for d in range(8):
                    wcs.load(src[128 * d:128 * (d + 1), :], dst[:, 1024 * d:1024 * (d + 1)])

        def load_expert_insw(e):
            for hc in range(2):
                for mat_e, mat_s, dst in ((w_in_e, w_in_s, wbfA), (w_sw_e, w_sw_s, wbfB)):
                    for d in range(8):
                        if e < NE:
                            sl = mat_e[e, 128 * d:128 * (d + 1), 1024 * hc:1024 * (hc + 1)]
                        else:
                            sl = mat_s[128 * d:128 * (d + 1), 1024 * hc:1024 * (hc + 1)]
                        wcs.load(sl, dst[:, 2048 * d + 1024 * hc:2048 * d + 1024 * (hc + 1)])

        def load_expert_wout(e):
            for f in range(16):
                sl = (w_out_e[e, 128 * f:128 * (f + 1), :] if e < NE
                      else w_out_s[128 * f:128 * (f + 1), :])
                ocs.load(sl, wod[:, 1024 * f:1024 * (f + 1)])

        def a1_stats(t):
            if t < 8:
                xt = xh[:, 1024 * t:1024 * (t + 1)]
            else:
                xt = xa32[:, 1024 * (t % 2):1024 * (t % 2 + 1)]
            sx = "sLXa" if t % 2 == 0 else "sLXb"
            if t >= 2:
                p.wait("scalar", "sVE", vn_mark[t - 2])
            p.dma("scalar", xt, x_in[128 * t:128 * (t + 1), :], sx)
            p.wait("vector", sx, p.c[sx])
            par = t % 4
            b12 = bn12w[:, 12 * par:12 * par + 12]
            mv = mvv[:, 2 * par:2 * par + 2]
            stat = st1[:, 2 * t:2 * t + 2]
            p.gop("vector", lambda E, xt=xt, b12=b12: E.bn_stats(
                out=b12[:, 0:6], in_=xt[:, 0:512]))
            p.gop("vector", lambda E, xt=xt, b12=b12: E.bn_stats(
                out=b12[:, 6:12], in_=xt[:, 512:1024]))
            p.gop("vector", lambda E, b12=b12, mv=mv: E.bn_aggr(out=mv, in_=b12))
            p.gop("vector", lambda E, mv=mv, stat=stat: E.tensor_scalar(
                out=stat[:, 0:1], in0=mv[:, 0:1], scalar1=-1.0, scalar2=None, op0=ALU.mult))
            vv = p.gop("vector", lambda E, mv=mv: E.tensor_scalar(
                out=mv[:, 1:2], in0=mv[:, 1:2], scalar1=LNEPS, scalar2=None, op0=ALU.add))
            p.wait("scalar", "sVE", vv)
            p.gop("scalar", lambda E, mv=mv: E.activation(
                out=mv[:, 1:2], in_=mv[:, 1:2], func=AF.Ln))
            va = p.gop("scalar", lambda E, mv=mv, stat=stat: E.activation(
                out=stat[:, 1:2], in_=mv[:, 1:2], func=AF.Exp, scale=-0.5))
            return va, xt

        p.wait("sync", "sVE", vBIAS)
        load_attn_weights(full=True)
        kv_mark = {}
        kxel_mark = {}
        vn_mark = {}
        evA = {}
        va_m = {}
        xts = {}
        tp_m = {}
        va_m[0], xts[0] = a1_stats(0)
        va_m[1], xts[1] = a1_stats(1)
        for t in range(NT):
            lt = lnts[t % 2]
            stat = st1[:, 2 * t:2 * t + 2]
            p.wait("vector", "sAC", va_m[t])
            if t >= 2:
                p.wait("vector", "sPE", tp_m[t - 2])
            vn = p.op("vector", lambda E, xt=xts[t], lt=lt, s=stat: E.tensor_scalar(
                out=lt, in0=xt, scalar1=s[:, 0:1], scalar2=s[:, 1:2],
                op0=ALU.add, op1=ALU.mult), inc="sVE")
            vn_mark[t] = vn
            if t + 2 < NT:
                va_m[t + 2], xts[t + 2] = a1_stats(t + 2)
            p.wait("tensor", "sVE", vn)
            if t == 0:
                p.wait("tensor", "sVE", vthr2)
            bb = 3 if t % 2 == 0 else 1
            if t >= 2:
                p.wait("tensor", "sAC", evA[t - 2])
            for c in range(8):
                vtp = p.op("tensor", lambda E, c=c, lt=lt, bb=bb: E.transpose(
                    out=bank_bf(bb)[:, 128 * c:128 * (c + 1)], in_=lt[:, 128 * c:128 * (c + 1)],
                    identity=identb[:]), inc="sPE" if c == 7 else None)
            tp_m[t] = vtp
            p.wait("scalar", "sPE", vtp)
            evA[t] = p.op("scalar", lambda E, t=t, bb=bb: E.copy(
                out=wod[:, 1024 * t:1024 * (t + 1)], in_=bank_bf(bb)[:]), inc="sAC")
            cw_kv = wcs.drain(1)

        for t in range(NT):
            if t == 0:
                p.wait_marks("tensor", cw_kv)
                p.wait("tensor", "sPO", vMEMS)
                p.wait("tensor", "sAC", evA[NT - 1])
            for h2 in range(2):
                for d in range(8):
                    p.op("tensor", lambda E, t=t, d=d, h2=h2: E.matmul(
                        out=bank(4 + h2),
                        lhsT=wod[:, 1024 * t + 128 * d:1024 * t + 128 * (d + 1)],
                        rhs=wbfA[:, 8192 + 1024 * d + 512 * h2:8192 + 1024 * d + 512 * (h2 + 1)],
                        start=(d == 0), stop=False))
                p.op("tensor", lambda E, h2=h2: E.matmul(
                    out=bank(4 + h2), lhsT=onesb[32:33, 0:128],
                    rhs=biasr[32:33, 512 * h2:512 * (h2 + 1)], start=False, stop=True))
                for d in range(8):
                    p.op("tensor", lambda E, t=t, d=d, h2=h2: E.matmul(
                        out=bank(6 + h2),
                        lhsT=wod[:, 1024 * t + 128 * d:1024 * t + 128 * (d + 1)],
                        rhs=wbfB[:, 1024 * d + 512 * h2:1024 * d + 512 * (h2 + 1)],
                        start=(d == 0), stop=False))
                vkv = p.op("tensor", lambda E, h2=h2: E.matmul(
                    out=bank(6 + h2), lhsT=onesb[64:65, 0:128],
                    rhs=biasr[64:65, 512 * h2:512 * (h2 + 1)], start=False, stop=True),
                    inc="sPE" if h2 == 1 else None)
            cwA = wcs.drain(2)

            kx, vxt = kxs[t % 2], vxs[t % 2]
            e_scr, r_scr = elu_scr[0], elu_scr[1]
            p.wait("scalar", "sPE", vkv)
            if t >= 1:
                p.wait("scalar", "sVE", kxel_mark[t - 1])
            p.op("scalar", lambda E, e_scr=e_scr: E.activation(
                out=e_scr, in_=PS[:, 2048:3072], func=AF.Exp), inc="sAC")
            vr_ = p.op("scalar", lambda E, r_scr=r_scr: E.activation(
                out=r_scr, in_=PS[:, 2048:3072], func=AF.Relu), inc="sAC")
            p.wait("vector", "sPE", vkv)
            if t >= 2:
                p.wait("vector", "sPE", kv_mark[t - 2])
            p.op("vector", lambda E, vxt=vxt: E.tensor_copy(
                out=vxt.rearrange("p (g c) -> p g c", c=65)[:, :, 0:64],
                in_=PS[:, 3072:4096].rearrange("p (g c) -> p g c", c=64)), inc="sVE")
            p.wait("vector", "sAC", vr_)
            kxel_mark[t] = p.op("vector", lambda E, kx=kx, e_scr=e_scr, r_scr=r_scr:
                                E.scalar_tensor_tensor(
                out=kx, in0=e_scr, scalar=1.0, in1=r_scr,
                op0=ALU.min, op1=ALU.add), inc="sVE")

            p.wait("tensor", "sVE", kxel_mark[t])
            for pr in range(NPAIR):
                b, co = kv_bank[pr], kv_col[pr]
                st_ = (t == 0) and (pr in (0, 3, 6))
                sp_ = (t == NT - 1) and (pr in (2, 5, 7))
                kv_mark[t] = p.op("tensor", lambda E, pr=pr, b=b, co=co, kx=kx, vxt=vxt,
                                  st_=st_, sp_=sp_: E.matmul(
                    out=PS[:, 512 * b + co:512 * b + co + 130],
                    lhsT=kx[:, 128 * pr:128 * (pr + 1)],
                    rhs=vxt[:, 130 * pr:130 * (pr + 1)],
                    start=st_, stop=sp_), inc="sPE" if pr == NPAIR - 1 else None)

        p.wait("vector", "sPE", kv_mark[NT - 1])
        for pr in range(NPAIR):
            base = 512 * kv_bank[pr] + kv_col[pr]
            p.op("vector", lambda E, pr=pr, base=base: E.tensor_copy(
                out=kvb[0:64, 128 * pr:128 * pr + 64], in_=PS[0:64, base:base + 64]), inc="sVE")
            p.op("vector", lambda E, pr=pr, base=base: E.tensor_copy(
                out=kvb[64:128, 128 * pr + 64:128 * pr + 128],
                in_=PS[64:128, base + 65:base + 129]), inc="sVE")
            p.op("vector", lambda E, pr=pr, base=base: E.tensor_copy(
                out=ksel[0:64, 18 * pr:18 * pr + 1], in_=PS[0:64, base + 64:base + 65]), inc="sVE")
            p.op("vector", lambda E, pr=pr, base=base: E.tensor_copy(
                out=ksel[64:128, 18 * pr + 1:18 * pr + 2],
                in_=PS[64:128, base + 129:base + 130]), inc="sVE")
        vKVB = p.c["sVE"]

        pk_raw = [scr8.bitcast(BF16)[:, 1024 * s:1024 * (s + 1)] for s in range(4)]
        pk_sems = ["sGA", "sGB", "sGC", "sGD"]
        gt_sems = ["sLXa", "sLXb", "sSELP", "sIX"]
        pk_scale_m = {}
        tp_consumed = {}

        def gather_tokens(e, jlo, jhi, first=False):
            p.wait("gpsimd", "sIX", vIX)
            p.wait("gpsimd", "sGF", p.c["sGF"])
            if first:
                p.wait("gpsimd", "sVE", vinva)
            for j in range(jlo, jhi):
                col = 5 * e + j
                sl = col % 4
                sga, sgc = pk_sems[sl], gt_sems[sl]
                if col - 4 in tp_consumed:
                    p.wait("gpsimd", "sPE", tp_consumed[col - 4])
                    p.wait("gpsimd", "sVE", pk_scale_m[col - 4])
                vg1_ = p.op("gpsimd", lambda E, sl=sl, col=col: E.indirect_dma_start(
                    out=pk_raw[sl], out_offset=None,
                    in_=ln2d[:], in_offset=bass.IndirectOffsetOnAxis(
                        ap=idxs[:, col:col + 1], axis=0)), inc=sga, n=16)
                vg2_ = p.op("gpsimd", lambda E, sl=sl, col=col: E.indirect_dma_start(
                    out=posa[:, 8 * sl:8 * sl + 8], out_offset=None,
                    in_=gatd[:], in_offset=bass.IndirectOffsetOnAxis(
                        ap=idxs[:, col:col + 1], axis=0)), inc=sgc, n=16)
                p.wait("vector", sga, vg1_)
                p.wait("vector", sgc, vg2_)
                pk_scale_m[col] = p.op("vector", lambda E, e=e, sl=sl: E.tensor_scalar(
                    out=pk_raw[sl], in0=pk_raw[sl],
                    scalar1=posa[:, 8 * sl + e:8 * sl + e + 1], scalar2=None,
                    op0=ALU.mult), inc="sVE")

        selp_free = 0
        for hf in range(2):
            toff = NTH * hf
            if hf == 1:
                p.wait("sync", "sPE", selp_free)
            p.dma("sync", selp[:], c_selp[:], "sSELP")
            vSELP = p.c["sSELP"]
            if hf == 1:
                for i in range(NTH):
                    t = toff + i
                    p.dma("sync", xh[:, 1024 * i:1024 * (i + 1)],
                          x_in[128 * t:128 * (t + 1), :], "sLXa" if i % 2 == 0 else "sLXb")
                load_attn_weights(full=False)
            vXHa, vXHb = p.c["sLXa"], p.c["sLXb"]

            dstf = ("sPE", p.c["sPE"]) if hf == 1 else None
            load_expert_insw(NE)
            load_expert_wout(NE)

            if hf == 1:
                p.wait("vector", "sLXa", vXHa)
                p.wait("vector", "sLXb", vXHb)
                for i in range(NTH):
                    t = toff + i
                    stat = st1[:, 2 * t:2 * t + 2]
                    lt = lnts[i % 2]
                    vn = p.op("vector", lambda E, i=i, lt=lt, s=stat: E.tensor_scalar(
                        out=lt, in0=xh[:, 1024 * i:1024 * (i + 1)], scalar1=s[:, 0:1],
                        scalar2=s[:, 1:2], op0=ALU.add, op1=ALU.mult), inc="sVE")
                    p.wait("tensor", "sVE", vn)
                    bb = 3 if i % 2 == 0 else 1
                    for c in range(8):
                        vtp = p.op("tensor", lambda E, c=c, lt=lt, bb=bb: E.transpose(
                            out=bank_bf(bb)[:, 128 * c:128 * (c + 1)],
                            in_=lt[:, 128 * c:128 * (c + 1)], identity=identb[:]),
                            inc="sPE" if c == 7 else None)
                    p.wait("vector", "sPE", vtp)
                    p.op("vector", lambda E, i=i, bb=bb: E.tensor_copy(
                        out=lnT.rearrange("p (c ti q) -> p c ti q", c=8, ti=8)[:, :, i, :],
                        in_=bank_bf(bb)[:].rearrange("p (c q) -> p c q", c=8)), inc="sVE")
                    wcs.drain(2, dst_free=dstf)
                vLN1T = p.c["sVE"]
                cwA = [(cs, p.c[cs]) for cs in wcs.cast_sems]
                p.wait("tensor", "sVE", vLN1T)

            wodT = wod.rearrange("p (ti c q) -> p c ti q", c=8, q=128)

            def qrhs(d, h2):
                if hf == 0:
                    return wodT[:, d, 4 * h2:4 * (h2 + 1), :]
                return lnT[:, 1024 * d + 512 * h2:1024 * d + 512 * (h2 + 1)]

            p.wait_marks("tensor", cwA)
            if hf == 0:
                p.wait("tensor", "sVE", vKVB)
            vq_evac = {}
            for o in range(8):
                g0 = 0 if o % 2 == 0 else 4
                if o >= 2:
                    p.wait("tensor", "sVE", vq_evac[o - 2])
                for d in range(8):
                    for h2 in range(2):
                        r_ap = qrhs(d, h2)
                        p.op("tensor", lambda E, o=o, d=d, h2=h2, g0=g0, r=r_ap: E.matmul(
                            out=bank(g0 + h2),
                            lhsT=wbfA[:, 1024 * d + 128 * o:1024 * d + 128 * o + 128],
                            rhs=r,
                            start=(d == 0), stop=False))
                for h2 in range(2):
                    vq = p.op("tensor", lambda E, o=o, h2=h2, g0=g0: E.matmul(
                        out=bank(g0 + h2), lhsT=biasr[0:1, 128 * o:128 * o + 128],
                        rhs=onesb[0:1, 0:512], start=False, stop=True),
                        inc="sPE" if h2 == 1 else None)
                p.wait("scalar", "sPE", vq)
                if o >= 1:
                    p.wait("scalar", "sVE", vq_evac[o - 1])
                p.op("scalar", lambda E, g0=g0: E.activation(
                    out=elu_scr[0], in_=PS[:, 512 * g0:512 * g0 + 1024], func=AF.Exp),
                    inc="sAC")
                vqr = p.op("scalar", lambda E, g0=g0: E.activation(
                    out=elu_scr[1], in_=PS[:, 512 * g0:512 * g0 + 1024], func=AF.Relu),
                    inc="sAC")
                p.wait("vector", "sAC", vqr)
                vq_evac[o] = p.op("vector", lambda E, o=o: E.scalar_tensor_tensor(
                    out=qh[:, 1024 * o:1024 * (o + 1)], in0=elu_scr[0], scalar=1.0,
                    in1=elu_scr[1], op0=ALU.min, op1=ALU.add), inc="sVE")
            p.wait("tensor", "sVE", vq_evac[7])

            p.wait("tensor", "sVE", vKVB)
            for pr in range(8):
                for h2 in range(2):
                    vZm = p.op("tensor", lambda E, pr=pr, h2=h2: E.matmul(
                        out=PS[0:16, 512 * (6 + h2):512 * (6 + h2) + 512],
                        lhsT=ksel[:, 16 * pr:16 * (pr + 1)],
                        rhs=qh[:, 1024 * pr + 512 * h2:1024 * pr + 512 * (h2 + 1)],
                        start=(pr == 0), stop=(pr == 7)),
                        inc="sPE" if (pr == 7 and h2 == 1) else None)
            p.wait("vector", "sPE", vZm)
            p.gop("vector", lambda E: E.tensor_scalar(
                out=zr1[:, 0:512], in0=PS[0:16, 3072:3584], scalar1=ZEPS, scalar2=None, op0=ALU.add))
            p.gop("vector", lambda E: E.tensor_scalar(
                out=zr1[:, 512:1024], in0=PS[0:16, 3584:4096], scalar1=ZEPS, scalar2=None, op0=ALU.add))
            p.gop("vector", lambda E: E.reciprocal(out=zr2[:], in_=zr1[:]))
            vZR = p.c["sVE"]

            p.wait("tensor", "sVE", vZR)
            p.wait("tensor", "sSELP", vSELP)
            last_evac = {}
            for pr in range(8):
                ob = 0 if pr % 2 == 0 else 4
                zb_b = 2 if pr % 2 == 0 else 6
                if pr >= 2:
                    p.wait("tensor", "sVE", last_evac[pr - 2])
                for h2 in range(2):
                    p.op("tensor", lambda E, pr=pr, h2=h2, ob=ob: E.matmul(
                        out=bank(ob + h2), lhsT=kvb[:, 128 * pr:128 * (pr + 1)],
                        rhs=qh[:, 1024 * pr + 512 * h2:1024 * pr + 512 * (h2 + 1)],
                        start=True, stop=True))
                    vs2 = p.op("tensor", lambda E, pr=pr, h2=h2, zb_b=zb_b: E.matmul(
                        out=bank(zb_b + h2), lhsT=selp[:, 128 * pr:128 * (pr + 1)],
                        rhs=zr2[:, 512 * h2:512 * (h2 + 1)], start=True, stop=True),
                        inc="sPE" if h2 == 1 else None)
                p.wait("scalar", "sPE", vs2)
                if pr >= 1:
                    p.wait("scalar", "sVE", last_evac[pr - 1])
                va = p.op("scalar", lambda E, zb_b=zb_b: E.copy(
                    out=zbuf, in_=PS[:, 512 * zb_b:512 * zb_b + 1024]), inc="sAC")
                p.wait("vector", "sAC", va)
                last_evac[pr] = p.op("vector", lambda E, pr=pr, ob=ob: E.tensor_tensor(
                    out=lnT[:, 1024 * pr:1024 * (pr + 1)],
                    in0=PS[:, 512 * ob:512 * ob + 1024], in1=zbuf, op=ALU.mult), inc="sVE")
            vATT = p.c["sVE"]
            selp_free = p.c["sPE"]

            p.wait("tensor", "sVE", vATT)
            last_x = {}
            for i in range(NTH):
                g0 = 0 if i % 2 == 0 else 4
                if i >= 2:
                    p.wait("tensor", "sVE", last_x[i - 2])
                for d in range(8):
                    for h2 in range(2):
                        p.op("tensor", lambda E, i=i, d=d, h2=h2, g0=g0: E.matmul(
                            out=bank(g0 + h2),
                            lhsT=lnT[:, 1024 * d + 128 * i:1024 * d + 128 * i + 128],
                            rhs=wbfB[:, 8192 + 1024 * d + 512 * h2:8192 + 1024 * d + 512 * (h2 + 1)],
                            start=(d == 0), stop=False))
                for h2 in range(2):
                    vo = p.op("tensor", lambda E, h2=h2, g0=g0: E.matmul(
                        out=bank(g0 + h2), lhsT=onesb[0:1, 0:128],
                        rhs=bo_row[:, 512 * h2:512 * (h2 + 1)],
                        start=False, stop=True), inc="sPE" if h2 == 1 else None)
                p.wait("vector", "sPE", vo)
                last_x[i] = p.op("vector", lambda E, i=i, g0=g0: E.scalar_tensor_tensor(
                    out=xh[:, 1024 * i:1024 * (i + 1)], in0=PS[:, 512 * g0:512 * g0 + 1024],
                    scalar=0.0, in1=xh[:, 1024 * i:1024 * (i + 1)],
                    op0=ALU.add, op1=ALU.add), inc="sVE")
                p.wait("vector", "sVE", last_x[i])
                p.op("vector", lambda E, i=i: E.bn_stats(
                    out=bn12w[:, 12 * i:12 * i + 6], in_=xh[:, 1024 * i:1024 * i + 512]))
                p.op("vector", lambda E, i=i: E.bn_stats(
                    out=bn12w[:, 12 * i + 6:12 * i + 12],
                    in_=xh[:, 1024 * i + 512:1024 * (i + 1)]),
                    inc="sVE" if i == NTH - 1 else None)
            vXATT = p.c["sVE"]

            dfree = ("sPE", p.c["sPE"])
            mark_hc0 = mark_hc1 = None

            p.wait("vector", "sVE", vXATT)
            for i in range(NTH):
                p.op("vector", lambda E, i=i: E.bn_aggr(
                    out=st2[:, 2 * i:2 * i + 2], in_=bn12w[:, 12 * i:12 * i + 12]),
                    inc="sVE" if i == NTH - 1 else None)
            p.wait("vector", "sVE", p.c["sVE"])
            sigb = mskb
            nmv = st2.rearrange("p (t two) -> p t two", two=2)
            p.gop("vector", lambda E: E.tensor_scalar(
                out=nmv[:, :, 0:1], in0=nmv[:, :, 0:1], scalar1=-1.0, scalar2=None, op0=ALU.mult))
            vv_ = p.gop("vector", lambda E: E.tensor_scalar(
                out=nmv[:, :, 1:2], in0=nmv[:, :, 1:2], scalar1=LNEPS, scalar2=None, op0=ALU.add))
            p.wait("scalar", "sVE", vv_)
            p.gop("scalar", lambda E: E.activation(
                out=nmv[:, :, 1:2], in_=nmv[:, :, 1:2], func=AF.Ln))
            va_ = p.gop("scalar", lambda E: E.activation(
                out=nmv[:, :, 1:2], in_=nmv[:, :, 1:2], func=AF.Exp, scale=-0.5))
            p.wait("vector", "sAC", va_)

            def ln2_norm(i):
                return p.op("vector", lambda E, i=i: E.tensor_scalar(
                    out=lnts[i % 2], in0=xh[:, 1024 * i:1024 * (i + 1)],
                    scalar1=st2[:, 2 * i:2 * i + 1], scalar2=st2[:, 2 * i + 1:2 * i + 2],
                    op0=ALU.add, op1=ALU.mult), inc="sVE")

            vn_m = {0: ln2_norm(0)}
            ln_dma_m = {}
            vevt_m = {}
            for i in range(NTH):
                g = NTH * hf + i
                lt = lnts[i % 2]
                sln = "sLNa" if i % 2 == 0 else "sLNb"
                p.wait("gpsimd", "sVE", vn_m[i])
                p.dma("gpsimd", ln2d[128 * g:128 * (g + 1), :], lt, sln)
                ln_dma_m[i] = p.c[sln]
                p.wait("tensor", "sVE", vn_m[i])
                if i >= 2:
                    p.wait("tensor", "sAC", vevt_m[i - 2])
                bb = 3 if i % 2 == 0 else 1
                sb_ = 2 if i % 2 == 0 else 0
                for c in range(8):
                    vtp = p.op("tensor", lambda E, c=c, lt=lt, bb=bb: E.transpose(
                        out=bank_bf(bb)[:, 128 * c:128 * (c + 1)],
                        in_=lt[:, 128 * c:128 * (c + 1)], identity=identb[:]),
                        inc="sPE" if c == 7 else None)
                p.wait("scalar", "sPE", vtp)
                vevt = p.op("scalar", lambda E, i=i, bb=bb: E.copy(
                    out=lnT.rearrange("p (c ti q) -> p c ti q", c=8, ti=8)[:, :, i, :],
                    in_=bank_bf(bb)[:].rearrange("p (c q) -> p c q", c=8)), inc="sAC")
                vevt_m[i] = vevt
                if i + 1 < NTH:
                    p.wait("vector", "sPE", vtp)
                    if i >= 1:
                        p.wait("vector", "sLNa" if (i + 1) % 2 == 0 else "sLNb",
                               ln_dma_m[i - 1])
                    vn_m[i + 1] = ln2_norm(i + 1)
                p.wait("tensor", "sAC", vevt)
                for d in range(8):
                    vsc = p.op("tensor", lambda E, i=i, d=d, sb_=sb_: E.matmul(
                        out=PS[:, 512 * sb_:512 * sb_ + 8],
                        lhsT=lnT[:, 1024 * d + 128 * i:1024 * d + 128 * i + 128],
                        rhs=routerb[:, 8 * d:8 * (d + 1)],
                        start=(d == 0), stop=(d == 7)), inc="sPE" if d == 7 else None)
                p.wait("vector", "sPE", vsc)
                p.gop("vector", lambda E, i=i, sb_=sb_: E.tensor_copy(
                    out=scall[:, 8 * i:8 * (i + 1)], in_=PS[:, 512 * sb_:512 * sb_ + 8]))
                mk = wcs.drain(6, dst_free=dfree)
                if i == 3:
                    mark_hc0 = mk
                if i == 7:
                    mark_hc1 = mk
            p.wait("vector", "sGF", p.c["sGF"])
            sc3 = scall.rearrange("p (t e) -> p t e", e=8)
            p.gop("vector", lambda E: E.tensor_reduce(
                out=mxb[:, 0:8], in_=sc3, axis=AX.X, op=ALU.max))
            p.gop("vector", lambda E: E.tensor_tensor(
                out=mskb.rearrange("p (t e) -> p t e", e=8), in0=sc3,
                in1=mxb[:, 0:8].rearrange("p (t o) -> p t o", o=1).broadcast_to([128, 8, 8]),
                op=ALU.is_ge))
            p.gop("vector", lambda E: E.scalar_tensor_tensor(
                out=mskb[:], in0=mskb[:], scalar=-1e30, in1=scall[:], op0=ALU.mult, op1=ALU.add))
            p.gop("vector", lambda E: E.tensor_reduce(
                out=mxb[:, 8:16], in_=mskb.rearrange("p (t e) -> p t e", e=8), axis=AX.X, op=ALU.max))
            p.gop("vector", lambda E: E.tensor_tensor(
                out=gatb.rearrange("p (t e) -> p t e", e=8), in0=sc3,
                in1=mxb[:, 8:16].rearrange("p (t o) -> p t o", o=1).broadcast_to([128, 8, 8]),
                op=ALU.is_ge))
            p.gop("vector", lambda E: E.tensor_copy(out=selm, in_=gatb[:]))
            vsel = p.gop("vector", lambda E: E.tensor_copy(out=gatbb[:], in_=selm))
            vg1 = p.c["sVE"]
            p.wait("scalar", "sVE", vg1)
            vsg = p.op("scalar", lambda E: E.activation(out=sigb[:], in_=scall[:], func=AF.Sigmoid),
                       inc="sAC")
            p.wait("vector", "sAC", vsg)
            vmu = p.op("vector", lambda E: E.tensor_tensor(
                out=gatb[:], in0=gatb[:], in1=sigb[:], op=ALU.mult), inc="sVE")
            p.wait("vector", "sVE", vmu)
            vism = p.gop("vector", lambda E: E.tensor_tensor(
                out=mskb.rearrange("p (t e) -> p t e", e=8), in0=sc3,
                in1=mxb[:, 0:8].rearrange("p (t o) -> p t o", o=1).broadcast_to([128, 8, 8]),
                op=ALU.is_ge))

            rt_ready = p.c["sVE"]
            if hf == 0:
                p.op("gpsimd", lambda E: bc_cell.__setitem__("r", E.to_reg(NSLOT - 1)) or E.nop())
            B4, B6 = 512 * 4, 512 * 6
            p.wait("tensor", "sVE", rt_ready)
            vtot = p.op("tensor", lambda E: E.matmul(
                out=PS[0:1, B6:B6 + 64], lhsT=onescol[:], rhs=gatbb[:, 0:64],
                start=True, stop=True), inc="sPE")
            for i in range(8):
                p.op("tensor", lambda E, i=i: E.matmul(
                    out=PS[:, B4 + 8 * i:B4 + 8 * (i + 1)], lhsT=tri_sb[:],
                    rhs=gatbb[:, 8 * i:8 * (i + 1)], start=(i == 0), stop=False))
            ta, tb_ = rtw[0:1, 128:192], rtw[0:1, 192:256]
            p.wait("vector", "sPE", vtot)
            p.gop("vector", lambda E: E.tensor_copy(out=rtw[0:1, 0:64], in_=PS[0:1, B6:B6 + 64]))
            p.gop("vector", lambda E: E.tensor_copy(out=ta[:, 0:64], in_=rtw[0:1, 0:64]))
            p.gop("vector", lambda E: E.tensor_copy(out=tb_[:, 0:8], in_=ta[:, 0:8]))
            p.gop("vector", lambda E: E.tensor_tensor(
                out=tb_[:, 8:64], in0=ta[:, 8:64], in1=ta[:, 0:56], op=ALU.add))
            p.gop("vector", lambda E: E.tensor_copy(out=ta[:, 0:16], in_=tb_[:, 0:16]))
            p.gop("vector", lambda E: E.tensor_tensor(
                out=ta[:, 16:64], in0=tb_[:, 16:64], in1=tb_[:, 0:48], op=ALU.add))
            p.gop("vector", lambda E: E.tensor_copy(out=tb_[:, 0:32], in_=ta[:, 0:32]))
            p.gop("vector", lambda E: E.tensor_tensor(
                out=tb_[:, 32:64], in0=ta[:, 32:64], in1=ta[:, 0:32], op=ALU.add))
            p.gop("vector", lambda E: E.tensor_tensor(
                out=tb_[:, 0:64], in0=tb_[:, 0:64], in1=rtw[0:1, 0:64], op=ALU.subtract))
            vcb = p.gop("vector", lambda E: E.tensor_tensor(
                out=rtw[0:1, 64:128].rearrange("o (t e) -> o t e", e=8),
                in0=tb_.rearrange("o (t e) -> o t e", e=8),
                in1=base_row[:].rearrange("o (t e) -> o t e", t=1).broadcast_to([1, 8, 8]),
                op=ALU.add))
            p.gop("vector", lambda E: E.tensor_tensor(
                out=base_row[:], in0=base_row[:], in1=tb_[:, 56:64], op=ALU.add))
            p.gop("vector", lambda E: E.tensor_tensor(
                out=base_row[:], in0=base_row[:], in1=rtw[0:1, 56:64], op=ALU.add))
            p.wait("tensor", "sVE", vcb)
            vps = p.op("tensor", lambda E: E.matmul(
                out=PS[:, B4:B4 + 64], lhsT=onesf[:], rhs=rtw[0:1, 64:128],
                start=False, stop=True), inc="sPE")
            ps_all = PS[:, B4:B4 + 64]
            p.wait("vector", "sPE", vps)
            if hf == 1:
                p.wait("vector", "sSC", p.c["sSC"])
            p.gop("vector", lambda E: E.tensor_tensor(
                out=scall.rearrange("p (t e) -> p t e", e=8),
                in0=ps_all.rearrange("p (t e) -> p t e", e=8),
                in1=thr_b[:].rearrange("p (t e) -> p t e", t=1).broadcast_to([128, 8, 8]),
                op=ALU.is_ge))
            p.gop("vector", lambda E: E.tensor_scalar(
                out=posa[:], in0=selm[:], scalar1=-1e9, scalar2=1e9,
                op0=ALU.mult, op1=ALU.add))
            p.gop("vector", lambda E: E.scalar_tensor_tensor(
                out=posa[:], in0=scall[:], scalar=1e9, in1=posa[:],
                op0=ALU.mult, op1=ALU.add))
            p.gop("vector", lambda E: E.tensor_tensor(
                out=posa[:], in0=posa[:], in1=ps_all, op=ALU.add))
            vposa = p.gop("vector", lambda E: E.tensor_copy(out=posia[:], in_=posa[:]))
            p.gop("vector", lambda E: E.tensor_tensor(
                out=scall[:], in0=mskb[:], in1=posa[:], op=ALU.mult))
            p.gop("vector", lambda E: E.tensor_reduce(
                out=mxb[:, 0:8], in_=scall.rearrange("p (t e) -> p t e", e=8),
                axis=AX.X, op=ALU.add))
            p.gop("vector", lambda E: E.tensor_tensor(
                out=scall[:], in0=selm[:], in1=mskb[:], op=ALU.subtract))
            p.gop("vector", lambda E: E.tensor_tensor(
                out=scall[:], in0=scall[:], in1=posa[:], op=ALU.mult))
            p.gop("vector", lambda E: E.tensor_reduce(
                out=mxb[:, 8:16], in_=scall.rearrange("p (t e) -> p t e", e=8),
                axis=AX.X, op=ALU.add))
            p.gop("vector", lambda E: E.tensor_single_scalar(
                out=mxb[:], in_=mxb[:], scalar=float(NSLOT), op=ALU.min))
            vinva = p.gop("vector", lambda E, hf=hf: E.tensor_copy(
                out=invs[:, 16 * hf:16 * (hf + 1)].rearrange("p (t two) -> p two t", two=2),
                in_=mxb[:].rearrange("p (two t) -> p two t", two=2)))
            if hf == 0:
                p.wait("gpsimd", "sLC", vLC)
            p.wait("gpsimd", "sVE", vposa)
            for i in range(8):
                g = NTH * hf + i
                for e_ in range(NE):
                    p.op("gpsimd", lambda E, e_=e_, i=i, g=g: E.indirect_dma_start(
                        out=idxd.rearrange("(a b) -> a b", b=1),
                        out_offset=bass.IndirectOffsetOnAxis(
                            ap=posia[:, 8 * i + e_:8 * i + e_ + 1], axis=0),
                        in_=tok_sb[:, g:g + 1], in_offset=None,
                        bounds_check=bc_cell["r"], oob_is_err=False), inc="sSC", n=16)
                p.dma("gpsimd", gatd[128 * g:128 * (g + 1), :], gatb[:, 8 * i:8 * (i + 1)],
                      "sGF")

            p.wait("vector", "sLNa", p.c["sLNa"])
            p.wait("vector", "sLNb", p.c["sLNb"])
            xo_mark = {}
            for tq in range(2):
                last_h = {}
                f15_mark = None
                for f in range(16):
                    mb = 0 if f % 2 == 0 else 2
                    sb_ = 1 if f % 2 == 0 else 3
                    if tq == 0 and f == 0:
                        p.wait_marks("tensor", mark_hc0)
                    if tq == 0 and f == 8:
                        p.wait_marks("tensor", mark_hc1)
                    if f >= 2:
                        p.wait("tensor", "sVE", last_h[f - 2])
                    for d in range(8):
                        rhs_ap = lnT[:, 1024 * d + 512 * tq:1024 * d + 512 * tq + 512]
                        p.op("tensor", lambda E, f=f, d=d, mb=mb, r=rhs_ap: E.matmul(
                            out=bank(mb),
                            lhsT=wbfA[:, 2048 * d + 128 * f:2048 * d + 128 * f + 128],
                            rhs=r, start=(d == 0), stop=(d == 7)))
                    for d in range(8):
                        rhs_ap = lnT[:, 1024 * d + 512 * tq:1024 * d + 512 * tq + 512]
                        vmm = p.op("tensor", lambda E, f=f, d=d, sb_=sb_, r=rhs_ap: E.matmul(
                            out=bank(sb_),
                            lhsT=wbfB[:, 2048 * d + 128 * f:2048 * d + 128 * f + 128],
                            rhs=r, start=(d == 0), stop=(d == 7)),
                            inc="sPE" if d == 7 else None)
                    if f == 15:
                        f15_mark = vmm
                    p.wait("scalar", "sPE", vmm)
                    vsi = p.op("scalar", lambda E, f=f, mb=mb: E.activation(
                        out=m2s[f % 2], in_=bank(mb), func=AF.Silu), inc="sAC")
                    p.wait("vector", "sAC", vsi)
                    last_h[f] = p.op("vector", lambda E, f=f, sb_=sb_: E.tensor_tensor(
                        out=qh[:, 512 * f:512 * (f + 1)], in0=m2s[f % 2], in1=bank(sb_),
                        op=ALU.mult), inc="sVE")
                vH = p.c["sVE"]

                p.wait("tensor", "sVE", vH)
                last_xo = {}
                for j in range(4):
                    i = 4 * tq + j
                    g0 = 4 if j % 2 == 0 else 6
                    if j >= 2:
                        p.wait("tensor", "sVE", last_xo[j - 2])
                    if hf == 1 and tq == 1 and wcs.pending:
                        mark_hc0 = wcs.drain(4, dst_free=("sPE", f15_mark))
                    for f in range(16):
                        for h2 in range(2):
                            vom = p.op("tensor", lambda E, f=f, j=j, h2=h2, g0=g0: E.matmul(
                                out=bank(g0 + h2),
                                lhsT=qh[:, 512 * f + 128 * j:512 * f + 128 * (j + 1)],
                                rhs=wod[:, 1024 * f + 512 * h2:1024 * f + 512 * (h2 + 1)],
                                start=(f == 0), stop=(f == 15)),
                                inc="sPE" if (f == 15 and h2 == 1) else None)
                    p.wait("vector", "sPE", vom)
                    last_xo[j] = p.op("vector", lambda E, i=i, g0=g0: E.scalar_tensor_tensor(
                        out=xh[:, 1024 * i:1024 * (i + 1)], in0=PS[:, 512 * g0:512 * g0 + 1024],
                        scalar=0.0, in1=xh[:, 1024 * i:1024 * (i + 1)],
                        op0=ALU.add, op1=ALU.add), inc="sVE")
                    xo_mark[i] = last_xo[j]
            if hf == 1:
                p.wait("gpsimd", "sSC", p.c["sSC"])
                p.wait("gpsimd", "sLNa", p.c["sLNa"])
                p.wait("gpsimd", "sLNb", p.c["sLNb"])
                for col in range(40):
                    e_, j_ = divmod(col, 5)
                    p.dma("gpsimd", idxs[:, col:col + 1],
                          idxd[CAP * e_ + 128 * j_:CAP * e_ + 128 * (j_ + 1)].rearrange(
                              "(a b) -> a b", b=1), "sIX")
                vIX = p.c["sIX"]
                gather_tokens(0, 0, 4, first=True)
            for i in range(NTH):
                t = toff + i
                p.wait("gpsimd", "sVE", xo_mark[i])
                p.dma("gpsimd", xmid[128 * t:128 * (t + 1), :], xh[:, 1024 * i:1024 * (i + 1)],
                      "sST")
            if hf == 0:
                p.wait("sync", "sST", p.c["sST"])

        sh_ph2_done = ("sPE", p.c["sPE"])
        xmid_done = p.c["sST"]
        load_expert_insw(0)
        load_expert_wout(0)

        m2sp = [m2[:, 0:640], gbuf[:, 0:640]]
        pout = [xh[:, 0:1024], xh[:, 1024:2048]]

        def h_ap(f, c0, c1):
            if f < 12:
                return lnT[:, 640 * f + c0:640 * f + c1]
            return qh[:, 5120 + 640 * (f - 12) + c0:5120 + 640 * (f - 12) + c1]

        def rt_ap(d, c0, c1):
            return qh[:, 640 * d + c0:640 * d + c1]

        pout_slot_mark = {0: 0, 1: 0}
        f15_prev = None
        for e in range(NE):
            sh_prev = ("sPE", p.c["sPE"])
            for j in range(5):
                col = 5 * e + j
                sl = col % 4
                p.wait("tensor", "sVE", pk_scale_m[col])
                if j >= 1:
                    p.wait("tensor", "sVE", trev)
                for c in range(8):
                    vtp = p.op("tensor", lambda E, sl=sl, c=c: E.transpose(
                        out=bank_bf(0)[:, 128 * c:128 * (c + 1)],
                        in_=pk_raw[sl][:, 128 * c:128 * (c + 1)], identity=identb[:]),
                        inc="sPE" if c == 7 else None)
                tp_consumed[col] = vtp
                if j == 0:
                    gather_tokens(e, 4, 5)
                p.wait("vector", "sPE", vtp)
                trev = p.op("vector", lambda E, j=j: E.tensor_copy(
                    out=qh[:, 0:5120].rearrange("p (d c) -> p d c", c=640)[:, :, 128 * j:128 * (j + 1)],
                    in_=bank_bf(0)[:].rearrange("p (d c) -> p d c", c=128)), inc="sVE")
                if e == 0 and j < 4:
                    mark_hc0 = wcs.drain(4, dst_free=sh_ph2_done)
            vrt = p.c["sVE"]
            if e + 1 < NE:
                load_expert_insw(e + 1)
                load_expert_wout(e + 1)
                gather_tokens(e + 1, 0, 4)

            p.wait("tensor", "sVE", vrt)
            last_h = {}
            f15_mark = None
            for f in range(16):
                base_c = 0 if f % 2 == 0 else 2048
                sw_c = 1024 if f % 2 == 0 else 3072
                if f == 0:
                    p.wait_marks("tensor", mark_hc0)
                if f == 8:
                    p.wait_marks("tensor", mark_hc1)
                if f >= 2:
                    p.wait("tensor", "sVE", last_h[f - 2])
                for mat, cbase in ((wbfA, base_c), (wbfB, sw_c)):
                    for d in range(8):
                        p.op("tensor", lambda E, f=f, d=d, mat=mat, cb=cbase: E.matmul(
                            out=PS[:, cb:cb + 512],
                            lhsT=mat[:, 2048 * d + 128 * f:2048 * d + 128 * f + 128],
                            rhs=rt_ap(d, 0, 512), start=(d == 0), stop=(d == 7)))
                    for d in range(8):
                        vmm = p.op("tensor", lambda E, f=f, d=d, mat=mat, cb=cbase: E.matmul(
                            out=PS[:, cb + 512:cb + 640],
                            lhsT=mat[:, 2048 * d + 128 * f:2048 * d + 128 * f + 128],
                            rhs=rt_ap(d, 512, 640), start=(d == 0), stop=(d == 7)),
                            inc="sPE" if (d == 7 and mat is wbfB) else None)
                if f == 15:
                    f15_mark = vmm
                p.wait("scalar", "sPE", vmm)
                vsi = p.op("scalar", lambda E, f=f, cb=base_c: E.activation(
                    out=m2sp[f % 2], in_=PS[:, cb:cb + 640], func=AF.Silu), inc="sAC")
                p.wait("vector", "sAC", vsi)
                last_h[f] = p.op("vector", lambda E, f=f, sc=sw_c: E.tensor_tensor(
                    out=h_ap(f, 0, 640), in0=m2sp[f % 2], in1=PS[:, sc:sc + 640],
                    op=ALU.mult), inc="sVE")
                if e == 0 and f < 8:
                    mk = wcs.drain(4, dst_free=sh_ph2_done)
                elif e > 0 and f < 14:
                    mk = wcs.drain(2, dst_free=(("sPE", f15_prev) if f < 6 else sh_prev))
                else:
                    mk = None
                if (e == 0 and f == 3) or (e > 0 and f == 5):
                    mark_hc1 = mk
                if (e == 0 and f == 7) or (e > 0 and f == 13):
                    wod_cur = mk
            vH = p.c["sVE"]

            f15_prev = f15_mark
            p.wait("tensor", "sVE", vH)
            p.wait_marks("tensor", wod_cur)
            last_po = {}
            for j in range(5):
                g0 = 4 if j % 2 == 0 else 6
                if j >= 2:
                    p.wait("tensor", "sVE", last_po[j - 2])
                if e == 0 and j == 0:
                    p.wait("vector", "sST", xmid_done)
                p.wait("vector", "sPa" if j % 2 == 0 else "sPb", pout_slot_mark[j % 2])
                if e + 1 < NE and wcs.pending:
                    mark_hc0 = wcs.drain(4, dst_free=("sPE", f15_mark))
                for f in range(16):
                    for h2 in range(2):
                        vom = p.op("tensor", lambda E, f=f, j=j, h2=h2, g0=g0: E.matmul(
                            out=bank(g0 + h2),
                            lhsT=h_ap(f, 128 * j, 128 * (j + 1)),
                            rhs=wod[:, 1024 * f + 512 * h2:1024 * f + 512 * (h2 + 1)],
                            start=(f == 0), stop=(f == 15)),
                            inc="sPE" if (f == 15 and h2 == 1) else None)
                p.wait("vector", "sPE", vom)
                last_po[j] = p.op("vector", lambda E, j=j, g0=g0: E.tensor_copy(
                    out=pout[j % 2], in_=PS[:, 512 * g0:512 * g0 + 1024]), inc="sVE")
                p.wait("gpsimd", "sVE", last_po[j])
                sp_ = "sPa" if j % 2 == 0 else "sPb"
                p.dma("gpsimd", poutd[CAP * e + 128 * j:CAP * e + 128 * (j + 1), :],
                      pout[j % 2], sp_)
                pout_slot_mark[j % 2] = p.c[sp_]
        vPa, vPb = p.c["sPa"], p.c["sPb"]

        xw = [wbfA.bitcast(F32)[:, 1024 * s:1024 * (s + 1)] for s in range(4)]
        ob1w = [wbfB.bitcast(F32)[:, 1024 * s:1024 * (s + 1)] for s in range(4)]
        ob2w = [wbfB.bitcast(F32)[:, 4096 + 1024 * s:4096 + 1024 * (s + 1)] for s in range(4)]
        xsem = ["sLXa", "sLXb", "sLNa", "sLNb"]
        g1sem = ["sGA", "sGB", "sGC", "sGD"]
        g2sem = ["sW0", "sW1", "sW2", "sW3"]
        ssem = ["sST", "sSELP", "sIX", "sLC"]
        wb1_mark = {}
        wb2_mark = {}
        st_mark = {}
        xl_mark = {}
        p.wait("sync", "sPE", f15_prev)
        p.wait("gpsimd", "sPE", f15_prev)
        p.wait("sync", "sST", xmid_done)
        for g in range(16):
            sl = g % 4
            if g >= 4:
                p.wait("sync", ssem[sl], st_mark[g - 4])
            p.dma("sync", xw[sl], xmid[128 * g:128 * (g + 1), :], xsem[sl])
            xl_mark[g] = p.c[xsem[sl]]
            if g == 0:
                p.wait("gpsimd", "sPa", vPa)
                p.wait("gpsimd", "sPb", vPb)
            if g >= 4:
                p.wait("gpsimd", "sVE", wb2_mark[g - 4])
            v1_ = p.op("gpsimd", lambda E, g=g, sl=sl: E.indirect_dma_start(
                out=ob1w[sl], out_offset=None,
                in_=poutd[:], in_offset=bass.IndirectOffsetOnAxis(
                    ap=invs[:, 2 * g:2 * g + 1], axis=0)), inc=g1sem[sl], n=16)
            v2_ = p.op("gpsimd", lambda E, g=g, sl=sl: E.indirect_dma_start(
                out=ob2w[sl], out_offset=None,
                in_=poutd[:], in_offset=bass.IndirectOffsetOnAxis(
                    ap=invs[:, 2 * g + 1:2 * g + 2], axis=0)), inc=g2sem[sl], n=16)
            p.wait("vector", g1sem[sl], v1_)
            p.wait("vector", xsem[sl], xl_mark[g])
            wb1_mark[g] = p.op("vector", lambda E, sl=sl: E.tensor_tensor(
                out=xw[sl], in0=xw[sl], in1=ob1w[sl], op=ALU.add), inc="sVE")
            p.wait("vector", g2sem[sl], v2_)
            p.wait("vector", "sVE", wb1_mark[g])
            wb2_mark[g] = p.op("vector", lambda E, sl=sl: E.tensor_tensor(
                out=xw[sl], in0=xw[sl], in1=ob2w[sl], op=ALU.add), inc="sVE")
            p.wait("scalar", "sVE", wb2_mark[g])
            p.dma("scalar", y_out[128 * g:128 * (g + 1), :], xw[sl], ssem[sl])
            st_mark[g] = p.c[ssem[sl]]
        for s4 in ssem:
            p.wait("gpsimd", s4, p.c[s4])

        with nc.Block() as block:
            @block.sync
            def _(E):
                for f in p.q["sync"]:
                    f(E)

            @block.tensor
            def _(E):
                for f in p.q["tensor"]:
                    f(E)

            @block.vector
            def _(E):
                for f in p.q["vector"]:
                    f(E)

            @block.scalar
            def _(E):
                for f in p.q["scalar"]:
                    f(E)

            @block.gpsimd
            def _(E):
                for f in p.q["gpsimd"]:
                    f(E)

    return nc


_CACHE = {}


def kernel(**inp):
    x = np.ascontiguousarray(np.asarray(inp["L0_attn_x"], dtype=np.float32))
    shared = {
        "wqT": np.ascontiguousarray(np.asarray(inp["L0_attn_Wq"], dtype=np.float32).T),
        "wkT": np.ascontiguousarray(np.asarray(inp["L0_attn_Wk"], dtype=np.float32).T),
        "wvT": np.ascontiguousarray(np.asarray(inp["L0_attn_Wv"], dtype=np.float32).T),
        "woT": np.ascontiguousarray(np.asarray(inp["L0_attn_Wo"], dtype=np.float32).T),
        "cbias": np.ascontiguousarray(np.stack([
            np.asarray(inp["L0_attn_bq"]), np.asarray(inp["L0_attn_bk"]),
            np.asarray(inp["L0_attn_bv"]), np.asarray(inp["L0_attn_bo"])]).astype(np.float32)),
        "router": np.ascontiguousarray(np.asarray(inp["L0_ffn_router_DE"], dtype=np.float32)),
        "w_in_e": np.ascontiguousarray(np.asarray(inp["L0_ffn_w_in_eDF"], dtype=np.float32)),
        "w_sw_e": np.ascontiguousarray(np.asarray(inp["L0_ffn_w_swiglu_eDF"], dtype=np.float32)),
        "w_out_e": np.ascontiguousarray(np.asarray(inp["L0_ffn_w_out_eFD"], dtype=np.float32)),
        "w_in_s": np.ascontiguousarray(np.asarray(inp["L0_ffn_w_in_shared_FD"], dtype=np.float32).T),
        "w_sw_s": np.ascontiguousarray(np.asarray(inp["L0_ffn_w_swiglu_FD"], dtype=np.float32).T),
        "w_out_s": np.ascontiguousarray(np.asarray(inp["L0_ffn_w_out_shared_DF"], dtype=np.float32).T),
        "c_id_bf": np.eye(128, dtype=ml_dtypes.bfloat16),
        "c_ones_bf": np.ones((1, 512), dtype=ml_dtypes.bfloat16),
        "c_tri": np.triu(np.ones((128, 128), dtype=np.float32), 1).T.astype(ml_dtypes.bfloat16),
        "c_onesf": np.ones((1, 128), dtype=np.float32),
        "c_tok": (np.arange(16, dtype=np.int32)[None, :] * 128
                  + np.arange(128, dtype=np.int32)[:, None]).astype(np.int32),
        "c_thr": ((np.arange(8, dtype=np.float32) + 1.0) * 640.0)[None, :],
        "c_base": (np.arange(8, dtype=np.float32) * 640.0)[None, :],
        "c_zidx": np.zeros((1, 8 * 640), dtype=np.int32),
    }
    selp = np.zeros((16, 1024), dtype=np.float32)
    for pr in range(8):
        for j in range(128):
            selp[2 * pr + (j // 64), 128 * pr + j] = 1.0
    shared["c_selp"] = selp

    if "nc" not in _CACHE:
        _CACHE["nc"] = build_program()
    nc = _CACHE["nc"]
    in_maps = [dict(x=np.ascontiguousarray(x[c]), **shared) for c in range(8)]
    res = run_bass_kernel_spmd(nc, in_maps, core_ids=list(range(8)))
    return np.stack([np.asarray(r["y"], dtype=np.float32) for r in res.results], axis=0)


# version 16
# speedup vs baseline: 1.3657x; 1.0360x over previous
import contextlib
from collections import defaultdict

import numpy as np
import ml_dtypes

import concourse.bass as bass
import concourse.mybir as mybir
from concourse.bass_utils import run_bass_kernel_spmd

F32 = mybir.dt.float32
BF16 = mybir.dt.bfloat16
AF = mybir.ActivationFunctionType
ALU = mybir.AluOpType
AX = mybir.AxisListType

T, D, FF, NE = 2048, 1024, 2048, 8
NT, NTH, NPAIR = 16, 8, 8
CAP = 640
NSLOT = NE * CAP
LNEPS, ZEPS = 1e-5, 1e-6
I32 = mybir.dt.int32

ENG_SEM = {"tensor": "sPE", "vector": "sVE", "scalar": "sAC", "gpsimd": "sPO"}


class P:
    def __init__(self, nc):
        self.nc = nc
        self.q = {k: [] for k in ("sync", "tensor", "vector", "scalar", "gpsimd")}
        self.sem = {}
        self.c = defaultdict(int)

    def mksem(self, ctx, *names):
        for n in names:
            self.sem[n] = ctx.enter_context(self.nc.semaphore(n))

    def wait(self, eng, sem, val=None):
        v = self.c[sem] if val is None else val
        if v <= 0:
            return
        h = self.sem[sem]
        self.q[eng].append(lambda E: E.wait_ge(h, v))

    def op(self, eng, fn, inc=None, n=1):
        if inc is None:
            self.q[eng].append(fn)
            return None
        self.c[inc] += n
        h = self.sem[inc]
        self.q[eng].append(lambda E: fn(E).then_inc(h, n))
        return self.c[inc]

    def gop(self, eng, fn):
        v = self.op(eng, fn, inc=ENG_SEM[eng])
        self.wait(eng, ENG_SEM[eng], v)
        return v

    def wait_marks(self, eng, marks):
        for sem, val in marks:
            self.wait(eng, sem, val)

    def dma(self, eng, out, in_, sem):
        return self.op(eng, lambda E: E.dma_start(out=out, in_=in_), inc=sem, n=16)


class CastStream:

    def __init__(self, p, slots, slot_sems, cast_sems, engines):
        self.p, self.slots, self.slot_sems, self.engines = p, slots, slot_sems, engines
        self.cast_sems = cast_sems
        self.k = 0
        self.kc = 0
        self.pending = []

    def load(self, src_ap, dst_ap):
        s = self.k % len(self.slots)
        if self.k >= len(self.slots):
            self.p.wait("sync", self.cast_sems[s], self.k // len(self.slots))
        self.p.dma("sync", self.slots[s], src_ap, self.slot_sems[s])
        self.pending.append((self.k, dst_ap, self.p.c[self.slot_sems[s]]))
        self.k += 1

    def drain(self, count=None, dst_free=None):
        p = self.p
        todo = self.pending if count is None else self.pending[:count]
        self.pending = self.pending[len(todo):]
        for idx, dst, dval in todo:
            eng = self.engines[self.kc % len(self.engines)]
            s = idx % len(self.slots)
            p.wait(eng, self.slot_sems[s], dval)
            if dst_free is not None:
                p.wait(eng, dst_free[0], dst_free[1])
            src = self.slots[s]
            if eng == "scalar":
                p.op(eng, lambda E, d=dst, sp=src: E.copy(out=d, in_=sp),
                     inc=self.cast_sems[s])
            else:
                p.op(eng, lambda E, d=dst, sp=src: E.tensor_copy(out=d, in_=sp),
                     inc=self.cast_sems[s])
            self.kc += 1
        return [(cs, p.c[cs]) for cs in self.cast_sems]


def build_program():
    nc = bass.Bass("TRN2", target_bir_lowering=False)

    def din(name, shape, dt=F32):
        return nc.dram_tensor(name, shape, dt, kind="ExternalInput").ap()

    x_in = din("x", [T, D])
    y_out = nc.dram_tensor("y", [T, D], F32, kind="ExternalOutput").ap()
    wqT, wkT, wvT, woT = (din(n, [D, D]) for n in ("wqT", "wkT", "wvT", "woT"))
    cbias = din("cbias", [4, D])
    router = din("router", [D, NE])
    w_in_e, w_sw_e = din("w_in_e", [NE, D, FF]), din("w_sw_e", [NE, D, FF])
    w_out_e = din("w_out_e", [NE, FF, D])
    w_in_s, w_sw_s = din("w_in_s", [D, FF]), din("w_sw_s", [D, FF])
    w_out_s = din("w_out_s", [FF, D])
    c_id_bf = din("c_id_bf", [128, 128], BF16)
    c_ones_bf = din("c_ones_bf", [1, 512], BF16)
    c_selp = din("c_selp", [16, 1024], F32)
    c_tri = din("c_tri", [128, 128], BF16)
    c_onesf = din("c_onesf", [1, 128], F32)
    c_tok = din("c_tok", [128, 16], I32)
    c_thr = din("c_thr", [1, 8], F32)
    c_base = din("c_base", [1, 8], F32)
    c_zidx = din("c_zidx", [1, NSLOT], I32)
    ln2d = nc.dram_tensor("ln2d", [T, D], BF16, kind="Internal").ap()
    gatd = nc.dram_tensor("gatd", [T, NE], F32, kind="Internal").ap()
    idxd = nc.dram_tensor("idxd", [NSLOT], I32, kind="Internal").ap()
    poutd = nc.dram_tensor("poutd", [NSLOT + 1, D], F32, kind="Internal").ap()
    xmid = nc.dram_tensor("xmid", [T, D], F32, kind="Internal").ap()

    with contextlib.ExitStack() as ctx:
        E_ = ctx.enter_context
        xh = E_(nc.sbuf_tensor("xh", [128, 8192], F32))
        lnT = E_(nc.sbuf_tensor("lnT", [128, 8192], BF16))
        qh = E_(nc.sbuf_tensor("qh", [128, 8192], BF16))
        wbfA = E_(nc.sbuf_tensor("wbfA", [128, 16384], BF16))
        wbfB = E_(nc.sbuf_tensor("wbfB", [128, 16384], BF16))
        wod = E_(nc.sbuf_tensor("wod", [128, 16384], BF16))
        landA = E_(nc.sbuf_tensor("landA", [128, 4096], F32))
        gbuf = E_(nc.sbuf_tensor("gbuf", [128, 1024], F32))
        m2 = E_(nc.sbuf_tensor("m2", [128, 1024], F32))
        scr8 = E_(nc.sbuf_tensor("scr8", [128, 2048], F32))
        kvb = E_(nc.sbuf_tensor("kvb", [128, 1024], BF16))
        ksel = E_(nc.sbuf_tensor("ksel", [128, 128], BF16))
        biasr = E_(nc.sbuf_tensor("biasr", [65, 1024], BF16))
        onesb = E_(nc.sbuf_tensor("onesb", [65, 512], BF16))
        identb = E_(nc.sbuf_tensor("identb", [128, 128], BF16))
        routerf = E_(nc.sbuf_tensor("routerf", [128, 64], F32))
        routerb = E_(nc.sbuf_tensor("routerb", [128, 64], BF16))
        bo_row = E_(nc.sbuf_tensor("bo_row", [1, 1024], BF16))
        tri_sb = E_(nc.sbuf_tensor("tri_sb", [128, 128], BF16))
        onesf = E_(nc.sbuf_tensor("onesf", [1, 128], F32))
        tok_sb = E_(nc.sbuf_tensor("tok_sb", [128, 16], I32))
        onescol = E_(nc.sbuf_tensor("onescol", [128, 1], BF16))
        thr_b = E_(nc.sbuf_tensor("thr_b", [128, 8], F32))
        base_row = E_(nc.sbuf_tensor("base_row", [1, 8], F32))
        posf = E_(nc.sbuf_tensor("posf", [128, 8], F32))
        posi = E_(nc.sbuf_tensor("posi", [128, 8], I32))
        idxs = E_(nc.sbuf_tensor("idxs", [128, 40], I32))
        invs = E_(nc.sbuf_tensor("invs", [128, 32], I32))
        invf = E_(nc.sbuf_tensor("invf", [128, 2], F32))
        st1 = E_(nc.sbuf_tensor("st1", [128, 32], F32))
        st2 = E_(nc.sbuf_tensor("st2", [128, 16], F32))
        bn12w = E_(nc.sbuf_tensor("bn12w", [128, 120], F32))
        bn12 = bn12w[:, 0:12]
        psev = bn12w[:, 96:104]
        psev_tot = bn12w[0:1, 104:112]
        rtmp = bn12w[:, 112:120]
        mvv = E_(nc.sbuf_tensor("mvv", [128, 8], F32))
        scall = E_(nc.sbuf_tensor("scall", [128, 64], F32))
        mskb = E_(nc.sbuf_tensor("mskb", [128, 64], F32))
        mxb = E_(nc.sbuf_tensor("mxb", [128, 16], F32))
        gatb = E_(nc.sbuf_tensor("gatb", [128, 64], F32))
        gatbb = E_(nc.sbuf_tensor("gatbb", [128, 64], BF16))
        rtw = E_(nc.sbuf_tensor("rtw", [1, 256], F32))
        posa = E_(nc.sbuf_tensor("posa", [128, 64], F32))
        posia = E_(nc.sbuf_tensor("posia", [128, 64], I32))
        PS = E_(nc.psum_tensor("PS", [128, 4096], F32))

        bc_cell = {}
        p = P(nc)
        p.mksem(ctx, "sLXa", "sLXb", "sW0", "sW1", "sW2", "sW3",
                "sC0", "sC1", "sC2", "sC3", "sLC", "sSELP", "sGF", "sIX", "sST",
                "sLNa", "sLNb", "sSC", "sGA", "sGB", "sGC", "sGD", "sPa", "sPb",
                "sPE", "sVE", "sAC", "sPO")

        def bank(b):
            return PS[:, 512 * b:512 * (b + 1)]

        def bank_bf(b):
            return PS[:, 512 * b:512 * (b + 1)].bitcast(BF16)

        elu_scr = [scr8[:, 0:1024], scr8[:, 1024:2048]]
        zbuf = scr8[:, 1024:2048]
        rt = scr8.bitcast(BF16)
        zr1 = m2[0:16, :]
        zr2 = scr8[0:16, 0:1024]
        selp = gbuf[0:16, :]
        gslot = [gbuf.bitcast(BF16)[:, 0:512], gbuf.bitcast(BF16)[:, 512:1024]]
        m2s = [m2[:, 0:512], m2[:, 512:1024]]
        kxs = [qh[:, 0:1024], qh[:, 1024:2048]]
        vxs = [qh[:, 2048:3088], qh[:, 3088:4128]]
        aT = qh[:, 4160:5184]
        xa32 = lnT.bitcast(F32)
        lnts = [qh[:, 5184:6208], qh[:, 6208:7232]]
        cb32 = landA[0:65, 0:1024]
        cbo = landA[0:1, 1024:2048]
        selm = bn12w[:, 0:64]
        kv_bank = [0, 0, 0, 1, 1, 1, 2, 2]
        kv_col = [0, 130, 260, 0, 130, 260, 0, 130]

        wcs = CastStream(p, [landA[:, 1024 * s:1024 * (s + 1)] for s in range(4)],
                         ["sW0", "sW1", "sW2", "sW3"], ["sC0", "sC1", "sC2", "sC3"],
                         ["vector", "scalar"])
        ocs = wcs

        p.dma("gpsimd", identb[:], c_id_bf[:], "sLC")
        p.dma("gpsimd", onesb[0:1, :], c_ones_bf[:], "sLC")
        p.dma("gpsimd", onesb[32:33, :], c_ones_bf[:], "sLC")
        p.dma("gpsimd", onesb[64:65, :], c_ones_bf[:], "sLC")
        p.dma("gpsimd", cb32[0:1, :], cbias[0:1, :], "sLC")
        p.dma("gpsimd", cb32[32:33, :], cbias[1:2, :], "sLC")
        p.dma("gpsimd", cb32[64:65, :], cbias[2:3, :], "sLC")
        p.dma("gpsimd", cbo[:], cbias[3:4, :], "sLC")
        for c in range(8):
            p.dma("gpsimd", routerf[:, 8 * c:8 * c + 8], router[128 * c:128 * (c + 1), :], "sLC")
        p.dma("gpsimd", tri_sb[:], c_tri[:], "sLC")
        p.dma("gpsimd", onesf[:], c_onesf[:], "sLC")
        p.dma("gpsimd", tok_sb[:], c_tok[:], "sLC")
        p.dma("gpsimd", onescol[:], c_ones_bf[0:1, 0:128].rearrange("o p -> p o"), "sLC")
        p.dma("gpsimd", thr_b[0:1, :], c_thr[:], "sLC")
        p.dma("gpsimd", base_row[:], c_base[:], "sLC")
        p.dma("gpsimd", idxd.rearrange("(a b) -> a b", a=1), c_zidx[:], "sLC")
        p.dma("gpsimd", poutd[NSLOT:NSLOT + 1, :], c_zidx[0:1, 0:1024].bitcast(F32), "sLC")
        vLC = p.c["sLC"]
        p.wait("vector", "sLC", vLC)
        p.op("vector", lambda E: E.tensor_copy(out=biasr[0:1, 0:1024], in_=cb32[0:1, :]), inc="sVE")
        p.op("vector", lambda E: E.tensor_copy(out=biasr[32:33, 0:1024], in_=cb32[32:33, :]), inc="sVE")
        p.op("vector", lambda E: E.tensor_copy(out=biasr[64:65, 0:1024], in_=cb32[64:65, :]), inc="sVE")
        p.op("vector", lambda E: E.tensor_copy(out=bo_row[:], in_=cbo), inc="sVE")
        vBIAS = p.op("vector", lambda E: E.tensor_copy(out=routerb[:], in_=routerf[:]), inc="sVE")
        p.wait("tensor", "sLC", vLC)
        vthr = p.op("tensor", lambda E: E.matmul(
            out=PS[:, 0:8], lhsT=onesf[0:1, :], rhs=thr_b[0:1, :],
            start=True, stop=True), inc="sPE")
        p.wait("vector", "sPE", vthr)
        vthr2 = p.op("vector", lambda E: E.tensor_copy(out=thr_b[:], in_=PS[:, 0:8]), inc="sVE")
        p.op("gpsimd", lambda E: E.memset(kvb[:], 0.0), inc="sPO")
        p.op("gpsimd", lambda E: E.memset(ksel[:], 0.0), inc="sPO")
        for s in range(2):
            p.op("gpsimd",
                 lambda E, s=s: E.memset(vxs[s].rearrange("p (g c) -> p g c", c=65)[:, :, 64:65], 1.0),
                 inc="sPO")
        vMEMS = p.c["sPO"]

        def load_attn_weights(full):
            mats = ((wkT, wbfA[:, 8192:16384]), (wvT, wbfB[:, 0:8192])) if full else ()
            mats = mats + ((wqT, wbfA[:, 0:8192]), (woT, wbfB[:, 8192:16384]))
            for src, dst in mats:
                for d in range(8):
                    wcs.load(src[128 * d:128 * (d + 1), :], dst[:, 1024 * d:1024 * (d + 1)])

        def load_expert_insw(e):
            for hc in range(2):
                for mat_e, mat_s, dst in ((w_in_e, w_in_s, wbfA), (w_sw_e, w_sw_s, wbfB)):
                    for d in range(8):
                        if e < NE:
                            sl = mat_e[e, 128 * d:128 * (d + 1), 1024 * hc:1024 * (hc + 1)]
                        else:
                            sl = mat_s[128 * d:128 * (d + 1), 1024 * hc:1024 * (hc + 1)]
                        wcs.load(sl, dst[:, 2048 * d + 1024 * hc:2048 * d + 1024 * (hc + 1)])

        def load_expert_wout(e):
            for f in range(16):
                sl = (w_out_e[e, 128 * f:128 * (f + 1), :] if e < NE
                      else w_out_s[128 * f:128 * (f + 1), :])
                ocs.load(sl, wod[:, 1024 * f:1024 * (f + 1)])

        def a1_stats(t):
            if t < 8:
                xt = xh[:, 1024 * t:1024 * (t + 1)]
            else:
                xt = xa32[:, 1024 * (t % 2):1024 * (t % 2 + 1)]
            sx = "sLXa" if t % 2 == 0 else "sLXb"
            if t >= 2:
                p.wait("scalar", "sVE", vn_mark[t - 2])
            p.dma("scalar", xt, x_in[128 * t:128 * (t + 1), :], sx)
            p.wait("vector", sx, p.c[sx])
            par = t % 4
            b12 = bn12w[:, 12 * par:12 * par + 12]
            mv = mvv[:, 2 * par:2 * par + 2]
            stat = st1[:, 2 * t:2 * t + 2]
            p.gop("vector", lambda E, xt=xt, b12=b12: E.bn_stats(
                out=b12[:, 0:6], in_=xt[:, 0:512]))
            p.gop("vector", lambda E, xt=xt, b12=b12: E.bn_stats(
                out=b12[:, 6:12], in_=xt[:, 512:1024]))
            p.gop("vector", lambda E, b12=b12, mv=mv: E.bn_aggr(out=mv, in_=b12))
            p.gop("vector", lambda E, mv=mv, stat=stat: E.tensor_scalar(
                out=stat[:, 0:1], in0=mv[:, 0:1], scalar1=-1.0, scalar2=None, op0=ALU.mult))
            vv = p.gop("vector", lambda E, mv=mv: E.tensor_scalar(
                out=mv[:, 1:2], in0=mv[:, 1:2], scalar1=LNEPS, scalar2=None, op0=ALU.add))
            p.wait("scalar", "sVE", vv)
            p.gop("scalar", lambda E, mv=mv: E.activation(
                out=mv[:, 1:2], in_=mv[:, 1:2], func=AF.Ln))
            va = p.gop("scalar", lambda E, mv=mv, stat=stat: E.activation(
                out=stat[:, 1:2], in_=mv[:, 1:2], func=AF.Exp, scale=-0.5))
            return va, xt

        p.wait("sync", "sVE", vBIAS)
        load_attn_weights(full=True)
        kv_mark = {}
        kxel_mark = {}
        vn_mark = {}
        evA = {}
        va_m = {}
        xts = {}
        tp_m = {}
        va_m[0], xts[0] = a1_stats(0)
        va_m[1], xts[1] = a1_stats(1)
        for t in range(NT):
            lt = lnts[t % 2]
            stat = st1[:, 2 * t:2 * t + 2]
            p.wait("vector", "sAC", va_m[t])
            if t >= 2:
                p.wait("vector", "sPE", tp_m[t - 2])
            vn = p.op("vector", lambda E, xt=xts[t], lt=lt, s=stat: E.tensor_scalar(
                out=lt, in0=xt, scalar1=s[:, 0:1], scalar2=s[:, 1:2],
                op0=ALU.add, op1=ALU.mult), inc="sVE")
            vn_mark[t] = vn
            if t + 2 < NT:
                va_m[t + 2], xts[t + 2] = a1_stats(t + 2)
            p.wait("tensor", "sVE", vn)
            if t == 0:
                p.wait("tensor", "sVE", vthr2)
            bb = 3 if t % 2 == 0 else 1
            if t >= 2:
                p.wait("tensor", evA[t - 2][0], evA[t - 2][1])
            for c in range(8):
                vtp = p.op("tensor", lambda E, c=c, lt=lt, bb=bb: E.transpose(
                    out=bank_bf(bb)[:, 128 * c:128 * (c + 1)], in_=lt[:, 128 * c:128 * (c + 1)],
                    identity=identb[:]), inc="sPE" if c == 7 else None)
            tp_m[t] = vtp
            if t % 2 == 0:
                p.wait("scalar", "sPE", vtp)
                evA[t] = ("sAC", p.op("scalar", lambda E, t=t, bb=bb: E.copy(
                    out=wod[:, 1024 * t:1024 * (t + 1)], in_=bank_bf(bb)[:]), inc="sAC"))
            else:
                p.wait("vector", "sPE", vtp)
                evA[t] = ("sVE", p.op("vector", lambda E, t=t, bb=bb: E.tensor_copy(
                    out=wod[:, 1024 * t:1024 * (t + 1)], in_=bank_bf(bb)[:]), inc="sVE"))
            cw_kv = wcs.drain(1)

        relu_m = {}
        vcp_m = {}

        def emit_pairs(t):
            p.wait("tensor", "sVE", kxel_mark[t])
            kx, vxt = kxs[t % 2], vxs[t % 2]
            for pr in range(NPAIR):
                b, co = kv_bank[pr], kv_col[pr]
                st_ = (t == 0) and (pr in (0, 3, 6))
                sp_ = (t == NT - 1) and (pr in (2, 5, 7))
                kv_mark[t] = p.op("tensor", lambda E, pr=pr, b=b, co=co, kx=kx, vxt=vxt,
                                  st_=st_, sp_=sp_: E.matmul(
                    out=PS[:, 512 * b + co:512 * b + co + 130],
                    lhsT=kx[:, 128 * pr:128 * (pr + 1)],
                    rhs=vxt[:, 130 * pr:130 * (pr + 1)],
                    start=st_, stop=sp_), inc="sPE" if pr == NPAIR - 1 else None)

        for t in range(NT):
            if t == 0:
                p.wait_marks("tensor", cw_kv)
                p.wait("tensor", "sPO", vMEMS)
                p.wait("tensor", evA[NT - 2][0], evA[NT - 2][1])
                p.wait("tensor", evA[NT - 1][0], evA[NT - 1][1])
            if t >= 1:
                p.wait("tensor", "sAC", relu_m[t - 1])
                p.wait("tensor", "sVE", vcp_m[t - 1])
            for h2 in range(2):
                for d in range(8):
                    p.op("tensor", lambda E, t=t, d=d, h2=h2: E.matmul(
                        out=bank(4 + h2),
                        lhsT=wod[:, 1024 * t + 128 * d:1024 * t + 128 * (d + 1)],
                        rhs=wbfA[:, 8192 + 1024 * d + 512 * h2:8192 + 1024 * d + 512 * (h2 + 1)],
                        start=(d == 0), stop=False))
                p.op("tensor", lambda E, h2=h2: E.matmul(
                    out=bank(4 + h2), lhsT=onesb[32:33, 0:128],
                    rhs=biasr[32:33, 512 * h2:512 * (h2 + 1)], start=False, stop=True))
                for d in range(8):
                    p.op("tensor", lambda E, t=t, d=d, h2=h2: E.matmul(
                        out=bank(6 + h2),
                        lhsT=wod[:, 1024 * t + 128 * d:1024 * t + 128 * (d + 1)],
                        rhs=wbfB[:, 1024 * d + 512 * h2:1024 * d + 512 * (h2 + 1)],
                        start=(d == 0), stop=False))
                vkv = p.op("tensor", lambda E, h2=h2: E.matmul(
                    out=bank(6 + h2), lhsT=onesb[64:65, 0:128],
                    rhs=biasr[64:65, 512 * h2:512 * (h2 + 1)], start=False, stop=True),
                    inc="sPE" if h2 == 1 else None)
            cwA = wcs.drain(2)

            if t >= 1:
                emit_pairs(t - 1)

            kx, vxt = kxs[t % 2], vxs[t % 2]
            e_scr, r_scr = elu_scr[0], elu_scr[1]
            p.wait("scalar", "sPE", vkv)
            if t >= 1:
                p.wait("scalar", "sVE", kxel_mark[t - 1])
            p.op("scalar", lambda E, e_scr=e_scr: E.activation(
                out=e_scr, in_=PS[:, 2048:3072], func=AF.Exp), inc="sAC")
            relu_m[t] = p.op("scalar", lambda E, r_scr=r_scr: E.activation(
                out=r_scr, in_=PS[:, 2048:3072], func=AF.Relu), inc="sAC")
            p.wait("vector", "sPE", vkv)
            if t >= 2:
                p.wait("vector", "sPE", kv_mark[t - 2])
            vcp_m[t] = p.op("vector", lambda E, vxt=vxt: E.tensor_copy(
                out=vxt.rearrange("p (g c) -> p g c", c=65)[:, :, 0:64],
                in_=PS[:, 3072:4096].rearrange("p (g c) -> p g c", c=64)), inc="sVE")
            p.wait("vector", "sAC", relu_m[t])
            kxel_mark[t] = p.op("vector", lambda E, kx=kx, e_scr=e_scr, r_scr=r_scr:
                                E.scalar_tensor_tensor(
                out=kx, in0=e_scr, scalar=1.0, in1=r_scr,
                op0=ALU.min, op1=ALU.add), inc="sVE")

        emit_pairs(NT - 1)
        p.wait("vector", "sPE", kv_mark[NT - 1])
        for pr in range(NPAIR):
            base = 512 * kv_bank[pr] + kv_col[pr]
            p.op("vector", lambda E, pr=pr, base=base: E.tensor_copy(
                out=kvb[0:64, 128 * pr:128 * pr + 64], in_=PS[0:64, base:base + 64]), inc="sVE")
            p.op("vector", lambda E, pr=pr, base=base: E.tensor_copy(
                out=kvb[64:128, 128 * pr + 64:128 * pr + 128],
                in_=PS[64:128, base + 65:base + 129]), inc="sVE")
            p.op("vector", lambda E, pr=pr, base=base: E.tensor_copy(
                out=ksel[0:64, 18 * pr:18 * pr + 1], in_=PS[0:64, base + 64:base + 65]), inc="sVE")
            p.op("vector", lambda E, pr=pr, base=base: E.tensor_copy(
                out=ksel[64:128, 18 * pr + 1:18 * pr + 2],
                in_=PS[64:128, base + 129:base + 130]), inc="sVE")
        vKVB = p.c["sVE"]

        pk_raw = [scr8.bitcast(BF16)[:, 1024 * s:1024 * (s + 1)] for s in range(4)]
        pk_sems = ["sGA", "sGB", "sGC", "sGD"]
        gt_sems = ["sLXa", "sLXb", "sSELP", "sIX"]
        pk_scale_m = {}
        tp_consumed = {}

        def gather_tokens(e, jlo, jhi, first=False):
            p.wait("gpsimd", "sIX", vIX)
            p.wait("gpsimd", "sGF", p.c["sGF"])
            if first:
                p.wait("gpsimd", "sVE", vinva)
            for j in range(jlo, jhi):
                col = 5 * e + j
                sl = col % 4
                sga, sgc = pk_sems[sl], gt_sems[sl]
                if col - 4 in tp_consumed:
                    p.wait("gpsimd", "sPE", tp_consumed[col - 4])
                    p.wait("gpsimd", "sVE", pk_scale_m[col - 4])
                vg1_ = p.op("gpsimd", lambda E, sl=sl, col=col: E.indirect_dma_start(
                    out=pk_raw[sl], out_offset=None,
                    in_=ln2d[:], in_offset=bass.IndirectOffsetOnAxis(
                        ap=idxs[:, col:col + 1], axis=0)), inc=sga, n=16)
                vg2_ = p.op("gpsimd", lambda E, sl=sl, col=col: E.indirect_dma_start(
                    out=posa[:, 8 * sl:8 * sl + 8], out_offset=None,
                    in_=gatd[:], in_offset=bass.IndirectOffsetOnAxis(
                        ap=idxs[:, col:col + 1], axis=0)), inc=sgc, n=16)
                p.wait("vector", sga, vg1_)
                p.wait("vector", sgc, vg2_)
                pk_scale_m[col] = p.op("vector", lambda E, e=e, sl=sl: E.tensor_scalar(
                    out=pk_raw[sl], in0=pk_raw[sl],
                    scalar1=posa[:, 8 * sl + e:8 * sl + e + 1], scalar2=None,
                    op0=ALU.mult), inc="sVE")

        selp_free = 0
        for hf in range(2):
            toff = NTH * hf
            if hf == 1:
                p.wait("sync", "sPE", selp_free)
            p.dma("sync", selp[:], c_selp[:], "sSELP")
            vSELP = p.c["sSELP"]
            if hf == 1:
                for i in range(NTH):
                    t = toff + i
                    p.dma("sync", xh[:, 1024 * i:1024 * (i + 1)],
                          x_in[128 * t:128 * (t + 1), :], "sLXa" if i % 2 == 0 else "sLXb")
                load_attn_weights(full=False)
            vXHa, vXHb = p.c["sLXa"], p.c["sLXb"]

            dstf = ("sPE", p.c["sPE"]) if hf == 1 else None
            load_expert_insw(NE)
            load_expert_wout(NE)

            if hf == 1:
                p.wait("vector", "sLXa", vXHa)
                p.wait("vector", "sLXb", vXHb)
                for i in range(NTH):
                    t = toff + i
                    stat = st1[:, 2 * t:2 * t + 2]
                    lt = lnts[i % 2]
                    vn = p.op("vector", lambda E, i=i, lt=lt, s=stat: E.tensor_scalar(
                        out=lt, in0=xh[:, 1024 * i:1024 * (i + 1)], scalar1=s[:, 0:1],
                        scalar2=s[:, 1:2], op0=ALU.add, op1=ALU.mult), inc="sVE")
                    p.wait("tensor", "sVE", vn)
                    bb = 3 if i % 2 == 0 else 1
                    for c in range(8):
                        vtp = p.op("tensor", lambda E, c=c, lt=lt, bb=bb: E.transpose(
                            out=bank_bf(bb)[:, 128 * c:128 * (c + 1)],
                            in_=lt[:, 128 * c:128 * (c + 1)], identity=identb[:]),
                            inc="sPE" if c == 7 else None)
                    p.wait("vector", "sPE", vtp)
                    p.op("vector", lambda E, i=i, bb=bb: E.tensor_copy(
                        out=lnT.rearrange("p (c ti q) -> p c ti q", c=8, ti=8)[:, :, i, :],
                        in_=bank_bf(bb)[:].rearrange("p (c q) -> p c q", c=8)), inc="sVE")
                    wcs.drain(2, dst_free=dstf)
                vLN1T = p.c["sVE"]
                cwA = [(cs, p.c[cs]) for cs in wcs.cast_sems]
                p.wait("tensor", "sVE", vLN1T)

            wodT = wod.rearrange("p (ti c q) -> p c ti q", c=8, q=128)

            def qrhs(d, h2):
                if hf == 0:
                    return wodT[:, d, 4 * h2:4 * (h2 + 1), :]
                return lnT[:, 1024 * d + 512 * h2:1024 * d + 512 * (h2 + 1)]

            p.wait_marks("tensor", cwA)
            if hf == 0:
                p.wait("tensor", "sVE", vKVB)
            vq_evac = {}
            for o in range(8):
                g0 = 0 if o % 2 == 0 else 4
                if o >= 2:
                    p.wait("tensor", "sVE", vq_evac[o - 2])
                for d in range(8):
                    for h2 in range(2):
                        r_ap = qrhs(d, h2)
                        p.op("tensor", lambda E, o=o, d=d, h2=h2, g0=g0, r=r_ap: E.matmul(
                            out=bank(g0 + h2),
                            lhsT=wbfA[:, 1024 * d + 128 * o:1024 * d + 128 * o + 128],
                            rhs=r,
                            start=(d == 0), stop=False))
                for h2 in range(2):
                    vq = p.op("tensor", lambda E, o=o, h2=h2, g0=g0: E.matmul(
                        out=bank(g0 + h2), lhsT=biasr[0:1, 128 * o:128 * o + 128],
                        rhs=onesb[0:1, 0:512], start=False, stop=True),
                        inc="sPE" if h2 == 1 else None)
                p.wait("scalar", "sPE", vq)
                if o >= 1:
                    p.wait("scalar", "sVE", vq_evac[o - 1])
                p.op("scalar", lambda E, g0=g0: E.activation(
                    out=elu_scr[0], in_=PS[:, 512 * g0:512 * g0 + 1024], func=AF.Exp),
                    inc="sAC")
                vqr = p.op("scalar", lambda E, g0=g0: E.activation(
                    out=elu_scr[1], in_=PS[:, 512 * g0:512 * g0 + 1024], func=AF.Relu),
                    inc="sAC")
                p.wait("vector", "sAC", vqr)
                vq_evac[o] = p.op("vector", lambda E, o=o: E.scalar_tensor_tensor(
                    out=qh[:, 1024 * o:1024 * (o + 1)], in0=elu_scr[0], scalar=1.0,
                    in1=elu_scr[1], op0=ALU.min, op1=ALU.add), inc="sVE")
            p.wait("tensor", "sVE", vq_evac[7])

            p.wait("tensor", "sVE", vKVB)
            for pr in range(8):
                for h2 in range(2):
                    vZm = p.op("tensor", lambda E, pr=pr, h2=h2: E.matmul(
                        out=PS[0:16, 512 * (6 + h2):512 * (6 + h2) + 512],
                        lhsT=ksel[:, 16 * pr:16 * (pr + 1)],
                        rhs=qh[:, 1024 * pr + 512 * h2:1024 * pr + 512 * (h2 + 1)],
                        start=(pr == 0), stop=(pr == 7)),
                        inc="sPE" if (pr == 7 and h2 == 1) else None)
            p.wait("vector", "sPE", vZm)
            p.gop("vector", lambda E: E.tensor_scalar(
                out=zr1[:, 0:512], in0=PS[0:16, 3072:3584], scalar1=ZEPS, scalar2=None, op0=ALU.add))
            p.gop("vector", lambda E: E.tensor_scalar(
                out=zr1[:, 512:1024], in0=PS[0:16, 3584:4096], scalar1=ZEPS, scalar2=None, op0=ALU.add))
            p.gop("vector", lambda E: E.reciprocal(out=zr2[:], in_=zr1[:]))
            vZR = p.c["sVE"]

            p.wait("tensor", "sVE", vZR)
            p.wait("tensor", "sSELP", vSELP)
            last_evac = {}
            for pr in range(8):
                ob = 0 if pr % 2 == 0 else 4
                zb_b = 2 if pr % 2 == 0 else 6
                if pr >= 2:
                    p.wait("tensor", "sVE", last_evac[pr - 2])
                for h2 in range(2):
                    p.op("tensor", lambda E, pr=pr, h2=h2, ob=ob: E.matmul(
                        out=bank(ob + h2), lhsT=kvb[:, 128 * pr:128 * (pr + 1)],
                        rhs=qh[:, 1024 * pr + 512 * h2:1024 * pr + 512 * (h2 + 1)],
                        start=True, stop=True))
                    vs2 = p.op("tensor", lambda E, pr=pr, h2=h2, zb_b=zb_b: E.matmul(
                        out=bank(zb_b + h2), lhsT=selp[:, 128 * pr:128 * (pr + 1)],
                        rhs=zr2[:, 512 * h2:512 * (h2 + 1)], start=True, stop=True),
                        inc="sPE" if h2 == 1 else None)
                p.wait("scalar", "sPE", vs2)
                if pr >= 1:
                    p.wait("scalar", "sVE", last_evac[pr - 1])
                va = p.op("scalar", lambda E, zb_b=zb_b: E.copy(
                    out=zbuf, in_=PS[:, 512 * zb_b:512 * zb_b + 1024]), inc="sAC")
                p.wait("vector", "sAC", va)
                last_evac[pr] = p.op("vector", lambda E, pr=pr, ob=ob: E.tensor_tensor(
                    out=lnT[:, 1024 * pr:1024 * (pr + 1)],
                    in0=PS[:, 512 * ob:512 * ob + 1024], in1=zbuf, op=ALU.mult), inc="sVE")
            vATT = p.c["sVE"]
            selp_free = p.c["sPE"]

            p.wait("tensor", "sVE", vATT)
            last_x = {}
            for i in range(NTH):
                g0 = 0 if i % 2 == 0 else 4
                if i >= 2:
                    p.wait("tensor", "sVE", last_x[i - 2])
                for d in range(8):
                    for h2 in range(2):
                        p.op("tensor", lambda E, i=i, d=d, h2=h2, g0=g0: E.matmul(
                            out=bank(g0 + h2),
                            lhsT=lnT[:, 1024 * d + 128 * i:1024 * d + 128 * i + 128],
                            rhs=wbfB[:, 8192 + 1024 * d + 512 * h2:8192 + 1024 * d + 512 * (h2 + 1)],
                            start=(d == 0), stop=False))
                for h2 in range(2):
                    vo = p.op("tensor", lambda E, h2=h2, g0=g0: E.matmul(
                        out=bank(g0 + h2), lhsT=onesb[0:1, 0:128],
                        rhs=bo_row[:, 512 * h2:512 * (h2 + 1)],
                        start=False, stop=True), inc="sPE" if h2 == 1 else None)
                p.wait("vector", "sPE", vo)
                last_x[i] = p.op("vector", lambda E, i=i, g0=g0: E.scalar_tensor_tensor(
                    out=xh[:, 1024 * i:1024 * (i + 1)], in0=PS[:, 512 * g0:512 * g0 + 1024],
                    scalar=0.0, in1=xh[:, 1024 * i:1024 * (i + 1)],
                    op0=ALU.add, op1=ALU.add), inc="sVE")
                p.wait("vector", "sVE", last_x[i])
                p.op("vector", lambda E, i=i: E.bn_stats(
                    out=bn12w[:, 12 * i:12 * i + 6], in_=xh[:, 1024 * i:1024 * i + 512]))
                p.op("vector", lambda E, i=i: E.bn_stats(
                    out=bn12w[:, 12 * i + 6:12 * i + 12],
                    in_=xh[:, 1024 * i + 512:1024 * (i + 1)]),
                    inc="sVE" if i == NTH - 1 else None)
            vXATT = p.c["sVE"]

            dfree = ("sPE", p.c["sPE"])
            mark_hc0 = mark_hc1 = None

            p.wait("vector", "sVE", vXATT)
            for i in range(NTH):
                p.op("vector", lambda E, i=i: E.bn_aggr(
                    out=st2[:, 2 * i:2 * i + 2], in_=bn12w[:, 12 * i:12 * i + 12]),
                    inc="sVE" if i == NTH - 1 else None)
            p.wait("vector", "sVE", p.c["sVE"])
            sigb = mskb
            nmv = st2.rearrange("p (t two) -> p t two", two=2)
            p.gop("vector", lambda E: E.tensor_scalar(
                out=nmv[:, :, 0:1], in0=nmv[:, :, 0:1], scalar1=-1.0, scalar2=None, op0=ALU.mult))
            vv_ = p.gop("vector", lambda E: E.tensor_scalar(
                out=nmv[:, :, 1:2], in0=nmv[:, :, 1:2], scalar1=LNEPS, scalar2=None, op0=ALU.add))
            p.wait("scalar", "sVE", vv_)
            p.gop("scalar", lambda E: E.activation(
                out=nmv[:, :, 1:2], in_=nmv[:, :, 1:2], func=AF.Ln))
            va_ = p.gop("scalar", lambda E: E.activation(
                out=nmv[:, :, 1:2], in_=nmv[:, :, 1:2], func=AF.Exp, scale=-0.5))
            p.wait("vector", "sAC", va_)

            def ln2_norm(i):
                return p.op("vector", lambda E, i=i: E.tensor_scalar(
                    out=lnts[i % 2], in0=xh[:, 1024 * i:1024 * (i + 1)],
                    scalar1=st2[:, 2 * i:2 * i + 1], scalar2=st2[:, 2 * i + 1:2 * i + 2],
                    op0=ALU.add, op1=ALU.mult), inc="sVE")

            vn_m = {0: ln2_norm(0)}
            ln_dma_m = {}
            vevt_m = {}
            for i in range(NTH):
                g = NTH * hf + i
                lt = lnts[i % 2]
                sln = "sLNa" if i % 2 == 0 else "sLNb"
                p.wait("gpsimd", "sVE", vn_m[i])
                p.dma("gpsimd", ln2d[128 * g:128 * (g + 1), :], lt, sln)
                ln_dma_m[i] = p.c[sln]
                p.wait("tensor", "sVE", vn_m[i])
                if i >= 2:
                    p.wait("tensor", vevt_m[i - 2][0], vevt_m[i - 2][1])
                bb = 3 if i % 2 == 0 else 1
                sb_ = 2 if i % 2 == 0 else 0
                for c in range(8):
                    vtp = p.op("tensor", lambda E, c=c, lt=lt, bb=bb: E.transpose(
                        out=bank_bf(bb)[:, 128 * c:128 * (c + 1)],
                        in_=lt[:, 128 * c:128 * (c + 1)], identity=identb[:]),
                        inc="sPE" if c == 7 else None)
                if i % 2 == 0:
                    p.wait("scalar", "sPE", vtp)
                    vevt_m[i] = ("sAC", p.op("scalar", lambda E, i=i, bb=bb: E.copy(
                        out=lnT.rearrange("p (c ti q) -> p c ti q", c=8, ti=8)[:, :, i, :],
                        in_=bank_bf(bb)[:].rearrange("p (c q) -> p c q", c=8)), inc="sAC"))
                else:
                    p.wait("vector", "sPE", vtp)
                    vevt_m[i] = ("sVE", p.op("vector", lambda E, i=i, bb=bb: E.tensor_copy(
                        out=lnT.rearrange("p (c ti q) -> p c ti q", c=8, ti=8)[:, :, i, :],
                        in_=bank_bf(bb)[:].rearrange("p (c q) -> p c q", c=8)), inc="sVE"))
                if i + 1 < NTH:
                    p.wait("vector", "sPE", vtp)
                    if i >= 1:
                        p.wait("vector", "sLNa" if (i + 1) % 2 == 0 else "sLNb",
                               ln_dma_m[i - 1])
                    vn_m[i + 1] = ln2_norm(i + 1)
                p.wait("tensor", vevt_m[i][0], vevt_m[i][1])
                for d in range(8):
                    vsc = p.op("tensor", lambda E, i=i, d=d, sb_=sb_: E.matmul(
                        out=PS[:, 512 * sb_:512 * sb_ + 8],
                        lhsT=lnT[:, 1024 * d + 128 * i:1024 * d + 128 * i + 128],
                        rhs=routerb[:, 8 * d:8 * (d + 1)],
                        start=(d == 0), stop=(d == 7)), inc="sPE" if d == 7 else None)
                p.wait("vector", "sPE", vsc)
                p.gop("vector", lambda E, i=i, sb_=sb_: E.tensor_copy(
                    out=scall[:, 8 * i:8 * (i + 1)], in_=PS[:, 512 * sb_:512 * sb_ + 8]))
                mk = wcs.drain(6, dst_free=dfree)
                if i == 3:
                    mark_hc0 = mk
                if i == 7:
                    mark_hc1 = mk
            p.wait("vector", "sGF", p.c["sGF"])
            sc3 = scall.rearrange("p (t e) -> p t e", e=8)
            p.gop("vector", lambda E: E.tensor_reduce(
                out=mxb[:, 0:8], in_=sc3, axis=AX.X, op=ALU.max))
            p.gop("vector", lambda E: E.tensor_tensor(
                out=mskb.rearrange("p (t e) -> p t e", e=8), in0=sc3,
                in1=mxb[:, 0:8].rearrange("p (t o) -> p t o", o=1).broadcast_to([128, 8, 8]),
                op=ALU.is_ge))
            p.gop("vector", lambda E: E.scalar_tensor_tensor(
                out=mskb[:], in0=mskb[:], scalar=-1e30, in1=scall[:], op0=ALU.mult, op1=ALU.add))
            p.gop("vector", lambda E: E.tensor_reduce(
                out=mxb[:, 8:16], in_=mskb.rearrange("p (t e) -> p t e", e=8), axis=AX.X, op=ALU.max))
            p.gop("vector", lambda E: E.tensor_tensor(
                out=gatb.rearrange("p (t e) -> p t e", e=8), in0=sc3,
                in1=mxb[:, 8:16].rearrange("p (t o) -> p t o", o=1).broadcast_to([128, 8, 8]),
                op=ALU.is_ge))
            p.gop("vector", lambda E: E.tensor_copy(out=selm, in_=gatb[:]))
            vsel = p.gop("vector", lambda E: E.tensor_copy(out=gatbb[:], in_=selm))
            vg1 = p.c["sVE"]
            p.wait("scalar", "sVE", vg1)
            vsg = p.op("scalar", lambda E: E.activation(out=sigb[:], in_=scall[:], func=AF.Sigmoid),
                       inc="sAC")
            p.wait("vector", "sAC", vsg)
            vmu = p.op("vector", lambda E: E.tensor_tensor(
                out=gatb[:], in0=gatb[:], in1=sigb[:], op=ALU.mult), inc="sVE")
            p.wait("vector", "sVE", vmu)
            vism = p.gop("vector", lambda E: E.tensor_tensor(
                out=mskb.rearrange("p (t e) -> p t e", e=8), in0=sc3,
                in1=mxb[:, 0:8].rearrange("p (t o) -> p t o", o=1).broadcast_to([128, 8, 8]),
                op=ALU.is_ge))

            rt_ready = p.c["sVE"]
            if hf == 0:
                p.op("gpsimd", lambda E: bc_cell.__setitem__("r", E.to_reg(NSLOT - 1)) or E.nop())
            B4, B6 = 512 * 4, 512 * 6
            p.wait("tensor", "sVE", rt_ready)
            vtot = p.op("tensor", lambda E: E.matmul(
                out=PS[0:1, B6:B6 + 64], lhsT=onescol[:], rhs=gatbb[:, 0:64],
                start=True, stop=True), inc="sPE")
            for i in range(8):
                p.op("tensor", lambda E, i=i: E.matmul(
                    out=PS[:, B4 + 8 * i:B4 + 8 * (i + 1)], lhsT=tri_sb[:],
                    rhs=gatbb[:, 8 * i:8 * (i + 1)], start=(i == 0), stop=False))
            ta, tb_ = rtw[0:1, 128:192], rtw[0:1, 192:256]
            p.wait("vector", "sPE", vtot)
            p.gop("vector", lambda E: E.tensor_copy(out=rtw[0:1, 0:64], in_=PS[0:1, B6:B6 + 64]))
            p.gop("vector", lambda E: E.tensor_copy(out=ta[:, 0:64], in_=rtw[0:1, 0:64]))
            p.gop("vector", lambda E: E.tensor_copy(out=tb_[:, 0:8], in_=ta[:, 0:8]))
            p.gop("vector", lambda E: E.tensor_tensor(
                out=tb_[:, 8:64], in0=ta[:, 8:64], in1=ta[:, 0:56], op=ALU.add))
            p.gop("vector", lambda E: E.tensor_copy(out=ta[:, 0:16], in_=tb_[:, 0:16]))
            p.gop("vector", lambda E: E.tensor_tensor(
                out=ta[:, 16:64], in0=tb_[:, 16:64], in1=tb_[:, 0:48], op=ALU.add))
            p.gop("vector", lambda E: E.tensor_copy(out=tb_[:, 0:32], in_=ta[:, 0:32]))
            p.gop("vector", lambda E: E.tensor_tensor(
                out=tb_[:, 32:64], in0=ta[:, 32:64], in1=ta[:, 0:32], op=ALU.add))
            p.gop("vector", lambda E: E.tensor_tensor(
                out=tb_[:, 0:64], in0=tb_[:, 0:64], in1=rtw[0:1, 0:64], op=ALU.subtract))
            vcb = p.gop("vector", lambda E: E.tensor_tensor(
                out=rtw[0:1, 64:128].rearrange("o (t e) -> o t e", e=8),
                in0=tb_.rearrange("o (t e) -> o t e", e=8),
                in1=base_row[:].rearrange("o (t e) -> o t e", t=1).broadcast_to([1, 8, 8]),
                op=ALU.add))
            p.gop("vector", lambda E: E.tensor_tensor(
                out=base_row[:], in0=base_row[:], in1=tb_[:, 56:64], op=ALU.add))
            p.gop("vector", lambda E: E.tensor_tensor(
                out=base_row[:], in0=base_row[:], in1=rtw[0:1, 56:64], op=ALU.add))
            p.wait("tensor", "sVE", vcb)
            vps = p.op("tensor", lambda E: E.matmul(
                out=PS[:, B4:B4 + 64], lhsT=onesf[:], rhs=rtw[0:1, 64:128],
                start=False, stop=True), inc="sPE")
            ps_all = PS[:, B4:B4 + 64]
            p.wait("vector", "sPE", vps)
            if hf == 1:
                p.wait("vector", "sSC", p.c["sSC"])
            p.gop("vector", lambda E: E.tensor_tensor(
                out=scall.rearrange("p (t e) -> p t e", e=8),
                in0=ps_all.rearrange("p (t e) -> p t e", e=8),
                in1=thr_b[:].rearrange("p (t e) -> p t e", t=1).broadcast_to([128, 8, 8]),
                op=ALU.is_ge))
            p.gop("vector", lambda E: E.tensor_scalar(
                out=posa[:], in0=selm[:], scalar1=-1e9, scalar2=1e9,
                op0=ALU.mult, op1=ALU.add))
            p.gop("vector", lambda E: E.scalar_tensor_tensor(
                out=posa[:], in0=scall[:], scalar=1e9, in1=posa[:],
                op0=ALU.mult, op1=ALU.add))
            p.gop("vector", lambda E: E.tensor_tensor(
                out=posa[:], in0=posa[:], in1=ps_all, op=ALU.add))
            vposa = p.gop("vector", lambda E: E.tensor_copy(out=posia[:], in_=posa[:]))
            p.gop("vector", lambda E: E.tensor_tensor(
                out=scall[:], in0=mskb[:], in1=posa[:], op=ALU.mult))
            p.gop("vector", lambda E: E.tensor_reduce(
                out=mxb[:, 0:8], in_=scall.rearrange("p (t e) -> p t e", e=8),
                axis=AX.X, op=ALU.add))
            p.gop("vector", lambda E: E.tensor_tensor(
                out=scall[:], in0=selm[:], in1=mskb[:], op=ALU.subtract))
            p.gop("vector", lambda E: E.tensor_tensor(
                out=scall[:], in0=scall[:], in1=posa[:], op=ALU.mult))
            p.gop("vector", lambda E: E.tensor_reduce(
                out=mxb[:, 8:16], in_=scall.rearrange("p (t e) -> p t e", e=8),
                axis=AX.X, op=ALU.add))
            p.gop("vector", lambda E: E.tensor_single_scalar(
                out=mxb[:], in_=mxb[:], scalar=float(NSLOT), op=ALU.min))
            vinva = p.gop("vector", lambda E, hf=hf: E.tensor_copy(
                out=invs[:, 16 * hf:16 * (hf + 1)].rearrange("p (t two) -> p two t", two=2),
                in_=mxb[:].rearrange("p (two t) -> p two t", two=2)))
            if hf == 0:
                p.wait("gpsimd", "sLC", vLC)
            p.wait("gpsimd", "sVE", vposa)
            for i in range(8):
                g = NTH * hf + i
                for e_ in range(NE):
                    p.op("gpsimd", lambda E, e_=e_, i=i, g=g: E.indirect_dma_start(
                        out=idxd.rearrange("(a b) -> a b", b=1),
                        out_offset=bass.IndirectOffsetOnAxis(
                            ap=posia[:, 8 * i + e_:8 * i + e_ + 1], axis=0),
                        in_=tok_sb[:, g:g + 1], in_offset=None,
                        bounds_check=bc_cell["r"], oob_is_err=False), inc="sSC", n=16)
                p.dma("gpsimd", gatd[128 * g:128 * (g + 1), :], gatb[:, 8 * i:8 * (i + 1)],
                      "sGF")

            p.wait("vector", "sLNa", p.c["sLNa"])
            p.wait("vector", "sLNb", p.c["sLNb"])
            xo_mark = {}
            for tq in range(2):
                last_h = {}
                f15_mark = None
                for f in range(16):
                    mb = 0 if f % 2 == 0 else 2
                    sb_ = 1 if f % 2 == 0 else 3
                    if tq == 0 and f == 0:
                        p.wait_marks("tensor", mark_hc0)
                    if tq == 0 and f == 8:
                        p.wait_marks("tensor", mark_hc1)
                    if f >= 2:
                        p.wait("tensor", "sVE", last_h[f - 2])
                    for d in range(8):
                        rhs_ap = lnT[:, 1024 * d + 512 * tq:1024 * d + 512 * tq + 512]
                        p.op("tensor", lambda E, f=f, d=d, mb=mb, r=rhs_ap: E.matmul(
                            out=bank(mb),
                            lhsT=wbfA[:, 2048 * d + 128 * f:2048 * d + 128 * f + 128],
                            rhs=r, start=(d == 0), stop=(d == 7)))
                    for d in range(8):
                        rhs_ap = lnT[:, 1024 * d + 512 * tq:1024 * d + 512 * tq + 512]
                        vmm = p.op("tensor", lambda E, f=f, d=d, sb_=sb_, r=rhs_ap: E.matmul(
                            out=bank(sb_),
                            lhsT=wbfB[:, 2048 * d + 128 * f:2048 * d + 128 * f + 128],
                            rhs=r, start=(d == 0), stop=(d == 7)),
                            inc="sPE" if d == 7 else None)
                    if f == 15:
                        f15_mark = vmm
                    p.wait("scalar", "sPE", vmm)
                    vsi = p.op("scalar", lambda E, f=f, mb=mb: E.activation(
                        out=m2s[f % 2], in_=bank(mb), func=AF.Silu), inc="sAC")
                    p.wait("vector", "sAC", vsi)
                    last_h[f] = p.op("vector", lambda E, f=f, sb_=sb_: E.tensor_tensor(
                        out=qh[:, 512 * f:512 * (f + 1)], in0=m2s[f % 2], in1=bank(sb_),
                        op=ALU.mult), inc="sVE")
                vH = p.c["sVE"]

                p.wait("tensor", "sVE", vH)
                last_xo = {}
                for j in range(4):
                    i = 4 * tq + j
                    g0 = 4 if j % 2 == 0 else 6
                    if j >= 2:
                        p.wait("tensor", "sVE", last_xo[j - 2])
                    if hf == 1 and tq == 1 and wcs.pending:
                        mark_hc0 = wcs.drain(4, dst_free=("sPE", f15_mark))
                    for f in range(16):
                        for h2 in range(2):
                            vom = p.op("tensor", lambda E, f=f, j=j, h2=h2, g0=g0: E.matmul(
                                out=bank(g0 + h2),
                                lhsT=qh[:, 512 * f + 128 * j:512 * f + 128 * (j + 1)],
                                rhs=wod[:, 1024 * f + 512 * h2:1024 * f + 512 * (h2 + 1)],
                                start=(f == 0), stop=(f == 15)),
                                inc="sPE" if (f == 15 and h2 == 1) else None)
                    p.wait("vector", "sPE", vom)
                    last_xo[j] = p.op("vector", lambda E, i=i, g0=g0: E.scalar_tensor_tensor(
                        out=xh[:, 1024 * i:1024 * (i + 1)], in0=PS[:, 512 * g0:512 * g0 + 1024],
                        scalar=0.0, in1=xh[:, 1024 * i:1024 * (i + 1)],
                        op0=ALU.add, op1=ALU.add), inc="sVE")
                    xo_mark[i] = last_xo[j]
            if hf == 1:
                p.wait("gpsimd", "sSC", p.c["sSC"])
                p.wait("gpsimd", "sLNa", p.c["sLNa"])
                p.wait("gpsimd", "sLNb", p.c["sLNb"])
                for col in range(40):
                    e_, j_ = divmod(col, 5)
                    p.dma("gpsimd", idxs[:, col:col + 1],
                          idxd[CAP * e_ + 128 * j_:CAP * e_ + 128 * (j_ + 1)].rearrange(
                              "(a b) -> a b", b=1), "sIX")
                vIX = p.c["sIX"]
                gather_tokens(0, 0, 4, first=True)
            for i in range(NTH):
                t = toff + i
                p.wait("gpsimd", "sVE", xo_mark[i])
                p.dma("gpsimd", xmid[128 * t:128 * (t + 1), :], xh[:, 1024 * i:1024 * (i + 1)],
                      "sST")
            if hf == 0:
                p.wait("sync", "sST", p.c["sST"])

        sh_ph2_done = ("sPE", p.c["sPE"])
        xmid_done = p.c["sST"]
        load_expert_insw(0)
        load_expert_wout(0)

        m2sp = [m2[:, 0:640], gbuf[:, 0:640]]
        pout = [xh[:, 0:1024], xh[:, 1024:2048]]

        def h_ap(f, c0, c1):
            if f < 12:
                return lnT[:, 640 * f + c0:640 * f + c1]
            return qh[:, 5120 + 640 * (f - 12) + c0:5120 + 640 * (f - 12) + c1]

        def rt_ap(d, c0, c1):
            return qh[:, 640 * d + c0:640 * d + c1]

        pout_slot_mark = {0: 0, 1: 0}
        f15_prev = None
        for e in range(NE):
            sh_prev = ("sPE", p.c["sPE"])
            for j in range(5):
                col = 5 * e + j
                sl = col % 4
                p.wait("tensor", "sVE", pk_scale_m[col])
                if j >= 1:
                    p.wait("tensor", "sVE", trev)
                for c in range(8):
                    vtp = p.op("tensor", lambda E, sl=sl, c=c: E.transpose(
                        out=bank_bf(0)[:, 128 * c:128 * (c + 1)],
                        in_=pk_raw[sl][:, 128 * c:128 * (c + 1)], identity=identb[:]),
                        inc="sPE" if c == 7 else None)
                tp_consumed[col] = vtp
                if j == 0:
                    gather_tokens(e, 4, 5)
                p.wait("vector", "sPE", vtp)
                trev = p.op("vector", lambda E, j=j: E.tensor_copy(
                    out=qh[:, 0:5120].rearrange("p (d c) -> p d c", c=640)[:, :, 128 * j:128 * (j + 1)],
                    in_=bank_bf(0)[:].rearrange("p (d c) -> p d c", c=128)), inc="sVE")
                if e == 0 and j < 4:
                    mark_hc0 = wcs.drain(4, dst_free=sh_ph2_done)
            vrt = p.c["sVE"]
            if e + 1 < NE:
                load_expert_insw(e + 1)
                load_expert_wout(e + 1)
                gather_tokens(e + 1, 0, 4)

            p.wait("tensor", "sVE", vrt)
            last_h = {}
            f15_mark = None
            for f in range(16):
                base_c = 0 if f % 2 == 0 else 2048
                sw_c = 1024 if f % 2 == 0 else 3072
                if f == 0:
                    p.wait_marks("tensor", mark_hc0)
                if f == 8:
                    p.wait_marks("tensor", mark_hc1)
                if f >= 2:
                    p.wait("tensor", "sVE", last_h[f - 2])
                for mat, cbase in ((wbfA, base_c), (wbfB, sw_c)):
                    for d in range(8):
                        p.op("tensor", lambda E, f=f, d=d, mat=mat, cb=cbase: E.matmul(
                            out=PS[:, cb:cb + 512],
                            lhsT=mat[:, 2048 * d + 128 * f:2048 * d + 128 * f + 128],
                            rhs=rt_ap(d, 0, 512), start=(d == 0), stop=(d == 7)))
                    for d in range(8):
                        vmm = p.op("tensor", lambda E, f=f, d=d, mat=mat, cb=cbase: E.matmul(
                            out=PS[:, cb + 512:cb + 640],
                            lhsT=mat[:, 2048 * d + 128 * f:2048 * d + 128 * f + 128],
                            rhs=rt_ap(d, 512, 640), start=(d == 0), stop=(d == 7)),
                            inc="sPE" if (d == 7 and mat is wbfB) else None)
                if f == 15:
                    f15_mark = vmm
                p.wait("scalar", "sPE", vmm)
                vsi = p.op("scalar", lambda E, f=f, cb=base_c: E.activation(
                    out=m2sp[f % 2], in_=PS[:, cb:cb + 640], func=AF.Silu), inc="sAC")
                p.wait("vector", "sAC", vsi)
                last_h[f] = p.op("vector", lambda E, f=f, sc=sw_c: E.tensor_tensor(
                    out=h_ap(f, 0, 640), in0=m2sp[f % 2], in1=PS[:, sc:sc + 640],
                    op=ALU.mult), inc="sVE")
                if e == 0 and f < 8:
                    mk = wcs.drain(4, dst_free=sh_ph2_done)
                elif e > 0 and f < 14:
                    mk = wcs.drain(2, dst_free=(("sPE", f15_prev) if f < 6 else sh_prev))
                else:
                    mk = None
                if (e == 0 and f == 3) or (e > 0 and f == 5):
                    mark_hc1 = mk
                if (e == 0 and f == 7) or (e > 0 and f == 13):
                    wod_cur = mk
            vH = p.c["sVE"]

            f15_prev = f15_mark
            p.wait("tensor", "sVE", vH)
            p.wait_marks("tensor", wod_cur)
            last_po = {}
            for j in range(5):
                g0 = 4 if j % 2 == 0 else 6
                if j >= 2:
                    p.wait("tensor", "sVE", last_po[j - 2])
                if e == 0 and j == 0:
                    p.wait("vector", "sST", xmid_done)
                p.wait("vector", "sPa" if j % 2 == 0 else "sPb", pout_slot_mark[j % 2])
                if e + 1 < NE and wcs.pending:
                    mark_hc0 = wcs.drain(4, dst_free=("sPE", f15_mark))
                for f in range(16):
                    for h2 in range(2):
                        vom = p.op("tensor", lambda E, f=f, j=j, h2=h2, g0=g0: E.matmul(
                            out=bank(g0 + h2),
                            lhsT=h_ap(f, 128 * j, 128 * (j + 1)),
                            rhs=wod[:, 1024 * f + 512 * h2:1024 * f + 512 * (h2 + 1)],
                            start=(f == 0), stop=(f == 15)),
                            inc="sPE" if (f == 15 and h2 == 1) else None)
                p.wait("vector", "sPE", vom)
                last_po[j] = p.op("vector", lambda E, j=j, g0=g0: E.tensor_copy(
                    out=pout[j % 2], in_=PS[:, 512 * g0:512 * g0 + 1024]), inc="sVE")
                p.wait("gpsimd", "sVE", last_po[j])
                sp_ = "sPa" if j % 2 == 0 else "sPb"
                p.dma("gpsimd", poutd[CAP * e + 128 * j:CAP * e + 128 * (j + 1), :],
                      pout[j % 2], sp_)
                pout_slot_mark[j % 2] = p.c[sp_]
        vPa, vPb = p.c["sPa"], p.c["sPb"]

        xw = [wbfA.bitcast(F32)[:, 1024 * s:1024 * (s + 1)] for s in range(4)]
        ob1w = [wbfB.bitcast(F32)[:, 1024 * s:1024 * (s + 1)] for s in range(4)]
        ob2w = [wbfB.bitcast(F32)[:, 4096 + 1024 * s:4096 + 1024 * (s + 1)] for s in range(4)]
        xsem = ["sLXa", "sLXb", "sLNa", "sLNb"]
        g1sem = ["sGA", "sGB", "sGC", "sGD"]
        g2sem = ["sW0", "sW1", "sW2", "sW3"]
        ssem = ["sST", "sSELP", "sIX", "sLC"]
        wb1_mark = {}
        wb2_mark = {}
        st_mark = {}
        xl_mark = {}
        p.wait("sync", "sPE", f15_prev)
        p.wait("gpsimd", "sPE", f15_prev)
        p.wait("sync", "sST", xmid_done)
        for g in range(16):
            sl = g % 4
            if g >= 4:
                p.wait("sync", ssem[sl], st_mark[g - 4])
            p.dma("sync", xw[sl], xmid[128 * g:128 * (g + 1), :], xsem[sl])
            xl_mark[g] = p.c[xsem[sl]]
            if g == 0:
                p.wait("gpsimd", "sPa", vPa)
                p.wait("gpsimd", "sPb", vPb)
            if g >= 4:
                p.wait("gpsimd", "sVE", wb2_mark[g - 4])
            v1_ = p.op("gpsimd", lambda E, g=g, sl=sl: E.indirect_dma_start(
                out=ob1w[sl], out_offset=None,
                in_=poutd[:], in_offset=bass.IndirectOffsetOnAxis(
                    ap=invs[:, 2 * g:2 * g + 1], axis=0)), inc=g1sem[sl], n=16)
            v2_ = p.op("gpsimd", lambda E, g=g, sl=sl: E.indirect_dma_start(
                out=ob2w[sl], out_offset=None,
                in_=poutd[:], in_offset=bass.IndirectOffsetOnAxis(
                    ap=invs[:, 2 * g + 1:2 * g + 2], axis=0)), inc=g2sem[sl], n=16)
            p.wait("vector", g1sem[sl], v1_)
            p.wait("vector", xsem[sl], xl_mark[g])
            wb1_mark[g] = p.op("vector", lambda E, sl=sl: E.tensor_tensor(
                out=xw[sl], in0=xw[sl], in1=ob1w[sl], op=ALU.add), inc="sVE")
            p.wait("vector", g2sem[sl], v2_)
            p.wait("vector", "sVE", wb1_mark[g])
            wb2_mark[g] = p.op("vector", lambda E, sl=sl: E.tensor_tensor(
                out=xw[sl], in0=xw[sl], in1=ob2w[sl], op=ALU.add), inc="sVE")
            p.wait("scalar", "sVE", wb2_mark[g])
            p.dma("scalar", y_out[128 * g:128 * (g + 1), :], xw[sl], ssem[sl])
            st_mark[g] = p.c[ssem[sl]]
        for s4 in ssem:
            p.wait("gpsimd", s4, p.c[s4])

        with nc.Block() as block:
            @block.sync
            def _(E):
                for f in p.q["sync"]:
                    f(E)

            @block.tensor
            def _(E):
                for f in p.q["tensor"]:
                    f(E)

            @block.vector
            def _(E):
                for f in p.q["vector"]:
                    f(E)

            @block.scalar
            def _(E):
                for f in p.q["scalar"]:
                    f(E)

            @block.gpsimd
            def _(E):
                for f in p.q["gpsimd"]:
                    f(E)

    return nc


_CACHE = {}


def kernel(**inp):
    x = np.ascontiguousarray(np.asarray(inp["L0_attn_x"], dtype=np.float32))
    shared = {
        "wqT": np.ascontiguousarray(np.asarray(inp["L0_attn_Wq"], dtype=np.float32).T),
        "wkT": np.ascontiguousarray(np.asarray(inp["L0_attn_Wk"], dtype=np.float32).T),
        "wvT": np.ascontiguousarray(np.asarray(inp["L0_attn_Wv"], dtype=np.float32).T),
        "woT": np.ascontiguousarray(np.asarray(inp["L0_attn_Wo"], dtype=np.float32).T),
        "cbias": np.ascontiguousarray(np.stack([
            np.asarray(inp["L0_attn_bq"]), np.asarray(inp["L0_attn_bk"]),
            np.asarray(inp["L0_attn_bv"]), np.asarray(inp["L0_attn_bo"])]).astype(np.float32)),
        "router": np.ascontiguousarray(np.asarray(inp["L0_ffn_router_DE"], dtype=np.float32)),
        "w_in_e": np.ascontiguousarray(np.asarray(inp["L0_ffn_w_in_eDF"], dtype=np.float32)),
        "w_sw_e": np.ascontiguousarray(np.asarray(inp["L0_ffn_w_swiglu_eDF"], dtype=np.float32)),
        "w_out_e": np.ascontiguousarray(np.asarray(inp["L0_ffn_w_out_eFD"], dtype=np.float32)),
        "w_in_s": np.ascontiguousarray(np.asarray(inp["L0_ffn_w_in_shared_FD"], dtype=np.float32).T),
        "w_sw_s": np.ascontiguousarray(np.asarray(inp["L0_ffn_w_swiglu_FD"], dtype=np.float32).T),
        "w_out_s": np.ascontiguousarray(np.asarray(inp["L0_ffn_w_out_shared_DF"], dtype=np.float32).T),
        "c_id_bf": np.eye(128, dtype=ml_dtypes.bfloat16),
        "c_ones_bf": np.ones((1, 512), dtype=ml_dtypes.bfloat16),
        "c_tri": np.triu(np.ones((128, 128), dtype=np.float32), 1).T.astype(ml_dtypes.bfloat16),
        "c_onesf": np.ones((1, 128), dtype=np.float32),
        "c_tok": (np.arange(16, dtype=np.int32)[None, :] * 128
                  + np.arange(128, dtype=np.int32)[:, None]).astype(np.int32),
        "c_thr": ((np.arange(8, dtype=np.float32) + 1.0) * 640.0)[None, :],
        "c_base": (np.arange(8, dtype=np.float32) * 640.0)[None, :],
        "c_zidx": np.zeros((1, 8 * 640), dtype=np.int32),
    }
    selp = np.zeros((16, 1024), dtype=np.float32)
    for pr in range(8):
        for j in range(128):
            selp[2 * pr + (j // 64), 128 * pr + j] = 1.0
    shared["c_selp"] = selp

    if "nc" not in _CACHE:
        _CACHE["nc"] = build_program()
    nc = _CACHE["nc"]
    in_maps = [dict(x=np.ascontiguousarray(x[c]), **shared) for c in range(8)]
    res = run_bass_kernel_spmd(nc, in_maps, core_ids=list(range(8)))
    return np.stack([np.asarray(r["y"], dtype=np.float32) for r in res.results], axis=0)
